# version 01
# baseline (speedup 1.0000x reference)
import sys
if "/opt/trn_rl_repo" not in sys.path:
    sys.path.insert(0, "/opt/trn_rl_repo")
import numpy as np
import ml_dtypes
from contextlib import ExitStack
import concourse.bass as bass
import concourse.mybir as mybir
from concourse.bass_utils import run_bass_kernel_spmd

F32 = mybir.dt.float32
BF16 = mybir.dt.bfloat16
AF = mybir.ActivationFunctionType
ALU = mybir.AluOpType
AX = mybir.AxisListType
BF = ml_dtypes.bfloat16

NC = 8
SEQ = 16384
D = 1024
DC = 8
T = SEQ // NC
TS = 512
NT = T // TS
NH = 8
DQK = 128
DV = 256
VDIM = 2048
FF = 4096
FC = FF // 128
EPS_LN = 1e-5
SC_QK = DQK ** -0.5
SWAP32 = [i ^ 1 for i in range(32)]

S_H = np.log2(1.0 - np.exp2(-5.0 - np.arange(NH, dtype=np.float64)))


class Sem:
    __slots__ = ("h", "c", "name")
    def __init__(self, h, name):
        self.h = h; self.c = 0; self.name = name
    def bump(self, n):
        self.c += n
        return self.c


class Prog:
    ENGS = ("sync", "gpsimd", "vector", "scalar", "tensor")
    def __init__(self, nc):
        self.nc = nc
        self.L = {e: [] for e in self.ENGS}

    def add(self, eng, fn):
        self.L[eng].append(fn)

    def wait(self, eng, sem, val):
        if val is None or val <= 0:
            return
        self.add(eng, lambda e, s=sem.h, v=val: e.wait_ge(s, v))

    def op(self, eng, fn, sem=None, n=1):
        if sem is None:
            self.add(eng, fn)
            return None
        v = sem.bump(n)
        self.add(eng, lambda e, f=fn, s=sem.h, k=n: f(e).then_inc(s, k))
        return v

    def dma(self, queue, sem, out, in_, n=16):
        v = sem.bump(n)
        self.add(queue, lambda e, o=out, i=in_, s=sem.h, k=n: e.dma_start(out=o, in_=i).then_inc(s, k))
        return v

    def emit(self, block):
        def run(e, lst):
            for f in lst:
                f(e)
        block.sync(lambda e: run(e, self.L["sync"]))
        block.gpsimd(lambda e: run(e, self.L["gpsimd"]))
        block.vector(lambda e: run(e, self.L["vector"]))
        block.scalar(lambda e: run(e, self.L["scalar"]))
        block.tensor(lambda e: run(e, self.L["tensor"]))


class Ctx:
    def __init__(self, nc):
        self.nc = nc
        self.p = Prog(nc)
        self.stack = ExitStack()
        self._semstack = ExitStack()
        self.sems = {}
        for name in ("qL", "gL", "T", "V", "SC", "G", "ST"):
            h = self._semstack.enter_context(nc.semaphore(name))
            self.sems[name] = Sem(h, name)
        self.qL = self.sems["qL"]; self.gL = self.sems["gL"]
        self.T_ = self.sems["T"]; self.V = self.sems["V"]
        self.SC = self.sems["SC"]; self.G = self.sems["G"]; self.ST = self.sems["ST"]
        self._n = 0

    def sb(self, shape, dt=F32, stack=None):
        self._n += 1
        return (stack or self.stack).enter_context(
            self.nc.sbuf_tensor(f"sb{self._n}", shape, dt))

    def ps(self, shape, dt=F32, stack=None):
        self._n += 1
        return (stack or self.stack).enter_context(
            self.nc.psum_tensor(f"ps{self._n}", shape, dt))

    def barrier(self):
        p = self.p
        vals = [(self.T_, self.T_.c), (self.V, self.V.c), (self.SC, self.SC.c),
                (self.G, self.G.c), (self.qL, self.qL.c), (self.gL, self.gL.c),
                (self.ST, self.ST.c)]
        for eng in Prog.ENGS:
            for s, v in vals:
                p.wait(eng, s, v)


def emit_ternarize(cx, W_dram, out_dram, rows, cols, scale_half, b, consts):
    p = cx.p; qL, V, T_, SC, ST = cx.qL, cx.V, cx.T_, cx.SC, cx.ST
    nrc = rows // 128
    scrA, scrB = b["scrA"], b["scrB"]
    for i in range(nrc):
        p.wait("sync", V, V.c)
        lv = p.dma("sync", qL, b["wch"][0:128, 0:cols], W_dram[i * 128:(i + 1) * 128, :])
        p.wait("vector", qL, lv)
        p.op("vector", lambda e, i=i, c=cols: e.reduce_sum(
            b["absacc"][:, i:i + 1], b["wch"][0:128, 0:c], axis=AX.X,
            apply_absolute_value=True), sem=V)
    p.wait("sync", V, V.c)
    wv = p.dma("sync", qL, scrA, b["absacc"][:, 0:nrc])
    p.wait("sync", qL, wv)
    lv = p.dma("sync", qL, b["flat"][0:1, 0:128 * nrc], scrA.rearrange("p n -> (p n)"))
    p.wait("vector", qL, lv)
    p.op("vector", lambda e, n=nrc: e.reduce_sum(b["mval"][0:1, 0:1], b["flat"][0:1, 0:128 * n], axis=AX.X), sem=V)
    p.wait("vector", V, V.c)
    p.op("vector", lambda e: e.tensor_scalar(b["mval"][0:1, 0:1], b["mval"][0:1, 0:1],
                                             1.0 / (rows * cols), None, ALU.mult), sem=V)
    p.wait("vector", V, V.c)
    p.op("vector", lambda e: e.tensor_scalar(b["mval"][0:1, 0:1], b["mval"][0:1, 0:1], 1e-8, None, ALU.max), sem=V)
    p.wait("vector", V, V.c)
    p.op("vector", lambda e: e.tensor_scalar(b["thr2"][0:1, 0:1], b["mval"][0:1, 0:1], 0.5, None, ALU.mult))
    p.op("vector", lambda e: e.tensor_scalar(b["thr2"][0:1, 1:2], b["mval"][0:1, 0:1], -0.5, None, ALU.mult), sem=V)
    p.wait("vector", V, V.c)
    p.wait("sync", V, V.c)
    wv = p.dma("sync", qL, scrB, b["thr2"][0:1, 0:2])
    p.wait("sync", qL, wv)
    p.dma("sync", qL, b["thrp"][:, 0:1], scrB[0:1, 0:1].to_broadcast([128, 1]))
    vthr = p.dma("sync", qL, b["thrn"][:, 0:1], scrB[0:1, 1:2].to_broadcast([128, 1]))
    for i in range(nrc):
        p.wait("sync", V, V.c)
        lv = p.dma("sync", qL, b["wch"][0:128, 0:cols], W_dram[i * 128:(i + 1) * 128, :])
        p.wait("vector", qL, lv)
        p.wait("vector", qL, vthr)
        p.wait("vector", ST, ST.c)
        p.op("vector", lambda e, c=cols: e.tensor_scalar(
            b["junk"][0:128, 0:c], b["wch"][0:128, 0:c], b["thrp"][:, 0:1], None, ALU.is_ge))
        p.op("vector", lambda e, c=cols: e.tensor_scalar(
            b["wch"][0:128, 0:c], b["wch"][0:128, 0:c], b["thrn"][:, 0:1], None, ALU.is_le))
        if scale_half:
            p.op("vector", lambda e, c=cols: e.tensor_sub(
                b["junk"][0:128, 0:c], b["junk"][0:128, 0:c], b["wch"][0:128, 0:c]))
            p.op("vector", lambda e, c=cols: e.tensor_scalar(
                b["stage"][0:128, 0:c], b["junk"][0:128, 0:c], 0.5, None, ALU.mult), sem=V)
        else:
            p.op("vector", lambda e, c=cols: e.tensor_sub(
                b["stage"][0:128, 0:c], b["junk"][0:128, 0:c], b["wch"][0:128, 0:c]), sem=V)
        p.wait("gpsimd", V, V.c)
        p.dma("gpsimd", ST, out_dram[:, i, 0:cols], b["stage"][0:128, 0:cols])


def emit_ln_stats(cx, x, b, consts, pR1, pR2):
    p = cx.p; qL, V, T_, SC = cx.qL, cx.V, cx.T_, cx.SC
    for j in range(NT):
        js = j * TS
        p.wait("vector", T_, T_.c)
        for dc in range(DC):
            p.op("vector", lambda e, dc=dc, js=js: e.tensor_copy(
                b["xb2"][:, dc, :], x[:, dc, js:js + TS]), sem=(V if dc == DC - 1 else None))
        vc = cx.V.c
        p.wait("scalar", V, vc)
        for dc in range(DC):
            p.op("scalar", lambda e, dc=dc: e.activation(
                b["xsq"][:, dc, :], b["xb2"][:, dc, :], AF.Square),
                sem=(SC if dc == DC - 1 else None))
        sc = cx.SC.c
        p.wait("tensor", V, vc)
        for dc in range(DC):
            p.op("tensor", lambda e, dc=dc: e.matmul(
                pR1[0:1, 0:TS], consts["onesd_b"][:, 0:1], b["xb2"][:, dc, :],
                start=(dc == 0), stop=(dc == DC - 1)))
        p.wait("tensor", SC, sc)
        for dc in range(DC):
            p.op("tensor", lambda e, dc=dc: e.matmul(
                pR2[0:1, 0:TS], consts["onesd_b"][:, 0:1], b["xsq"][:, dc, :],
                start=(dc == 0), stop=(dc == DC - 1)), sem=(T_ if dc == DC - 1 else None))
        p.wait("scalar", T_, T_.c)
        p.op("scalar", lambda e, js=js: e.copy(b["mrow"][0:1, js:js + TS], pR1[0:1, 0:TS]))
        p.op("scalar", lambda e, js=js: e.copy(b["qrow"][0:1, js:js + TS], pR2[0:1, 0:TS]), sem=SC)
        p.wait("tensor", SC, SC.c)
    p.wait("scalar", SC, SC.c)
    p.op("scalar", lambda e: e.activation(b["srow"][0:1, :], b["mrow"][0:1, :], AF.Square), sem=SC)
    p.wait("vector", SC, SC.c)
    p.op("vector", lambda e: e.scalar_tensor_tensor(b["qrow"][0:1, :], b["qrow"][0:1, :], EPS_LN,
                                                    b["srow"][0:1, :], ALU.add, ALU.subtract), sem=V)
    p.wait("scalar", V, V.c)
    p.op("scalar", lambda e: e.activation(b["qrow"][0:1, :], b["qrow"][0:1, :], AF.Ln), sem=SC)
    p.wait("scalar", SC, SC.c)
    p.op("scalar", lambda e: e.activation(b["srow"][0:1, :], b["qrow"][0:1, :], AF.Exp, scale=-0.5), sem=SC)
    p.wait("vector", SC, SC.c)
    p.op("vector", lambda e: e.tensor_mul(b["mrow"][0:1, :], b["mrow"][0:1, :], b["srow"][0:1, :]), sem=V)
    p.wait("sync", V, V.c)
    p.wait("sync", SC, SC.c)
    wv = p.dma("sync", qL, b["lnscr"][0:1, :], b["srow"][0:1, :])
    wv = p.dma("sync", qL, b["lnscr"][1:2, :], b["mrow"][0:1, :])
    p.wait("sync", qL, wv)
    p.dma("sync", qL, b["rstd_b"][:], b["lnscr"][0:1, :].to_broadcast([128, T]))
    bq = p.dma("sync", qL, b["mr_b"][:], b["lnscr"][1:2, :].to_broadcast([128, T]))
    b["bcast_ql"] = bq
    return bq


def emit_ln_norm_j(cx, x, xln_dst, j, b, dst_js=None):
    p = cx.p; V, qL = cx.V, cx.qL
    js = j * TS
    dj = js if dst_js is None else dst_js
    p.wait("vector", qL, b["bcast_ql"])
    for dc in range(DC):
        p.op("vector", lambda e, dc=dc, js=js, dj=dj: e.tensor_mul(
            xln_dst[:, dc, dj:dj + TS], x[:, dc, js:js + TS], b["rstd_b"][:, js:js + TS]))
        p.op("vector", lambda e, dc=dc, js=js, dj=dj: e.tensor_sub(
            xln_dst[:, dc, dj:dj + TS], xln_dst[:, dc, dj:dj + TS], b["mr_b"][:, js:js + TS]),
            sem=(V if dc == DC - 1 else None))
    return cx.V.c


def emit_ffn0(cx, x, b, consts, t1_dram, t2_dram, pA, st_ready, dbg=None):
    p = cx.p; qL, V, T_, SC, ST = cx.qL, cx.V, cx.T_, cx.SC, cx.ST
    last_w1 = {0: None, 1: None}
    last_w2 = {0: None, 1: None}
    emit_ln_stats(cx, x, b, consts, cx.pR1, cx.pR2)
    p.wait("vector", SC, b["sc_hb2pp"])
    for j in range(NT):
        js = j * TS
        vnorm = emit_ln_norm_j(cx, x, b["xlnj"], j, b, dst_js=0)
        p.wait("tensor", V, vnorm)
        if dbg is not None and j == 0:
            p.wait("sync", V, vnorm)
            p.dma("sync", cx.ST, dbg["xlnj"], b["xlnj"][:])
        for s4 in range(8):
            bi = s4 % 2
            if last_w1[bi] is None:
                p.wait("sync", ST, st_ready)
            else:
                p.wait("sync", T_, last_w1[bi])
            lv = p.dma("sync", qL, b["w1b"][:, bi, :, :], t1_dram[:, :, s4 * 512:(s4 + 1) * 512])
            p.wait("tensor", qL, lv)
            for fo in range(4):
                fc = s4 * 4 + fo
                p.wait("tensor", SC, cx.SC.c)
                for dc in range(DC):
                    p.op("tensor", lambda e, dc=dc, bi=bi, fo=fo: e.matmul(
                        pA[:, 0:TS], b["w1b"][:, bi, dc, fo * 128:(fo + 1) * 128],
                        b["xlnj"][:, dc, 0:TS],
                        start=(dc == 0), stop=(dc == DC - 1)), sem=(T_ if dc == DC - 1 else None))
                p.wait("scalar", T_, T_.c)
                p.op("scalar", lambda e, fc=fc: e.activation(
                    b["hb"][:, fc, :], pA[:, 0:TS], AF.Gelu,
                    bias=consts["b1pp"][:, fc:fc + 1]), sem=SC)
            last_w1[bi] = cx.T_.c
        if dbg is not None and j == 0:
            p.wait("sync", SC, cx.SC.c)
            p.dma("sync", cx.ST, dbg["hb"], b["hb"][:])
        p.wait("tensor", SC, cx.SC.c)
        for s4 in range(8):
            bi = s4 % 2
            dc = s4
            if last_w2[bi] is None:
                p.wait("sync", ST, st_ready)
            else:
                p.wait("sync", T_, last_w2[bi])
            lv = p.dma("sync", qL, b["w2b"][:, bi, :, :], t2_dram[:, :, s4 * 128:(s4 + 1) * 128])
            p.wait("tensor", qL, lv)
            p.wait("tensor", V, cx.V.c)
            for fc in range(FC):
                p.op("tensor", lambda e, fc=fc, bi=bi: e.matmul(
                    pA[:, 0:TS], b["w2b"][:, bi, fc, :],
                    b["hb"][:, fc, :],
                    start=(fc == 0), stop=(fc == FC - 1)), sem=(T_ if fc == FC - 1 else None))
            last_w2[bi] = cx.T_.c
            p.wait("vector", T_, T_.c)
            p.op("vector", lambda e, dc=dc, js=js: e.scalar_tensor_tensor(
                x[:, dc, js:js + TS], pA[:, 0:TS], b["hb2pp"][:, dc:dc + 1],
                x[:, dc, js:js + TS], ALU.add, ALU.add), sem=V)
        p.wait("tensor", V, cx.V.c)


def ln_bufs(cx, stack):
    return dict(
        xb2=cx.sb([128, DC, TS], BF16, stack), xsq=cx.sb([128, DC, TS], BF16, stack),
        mrow=cx.sb([1, T], F32, stack), qrow=cx.sb([1, T], F32, stack),
        srow=cx.sb([1, T], F32, stack),
    )


def build_L1():
    nc = bass.Bass("TRN2", target_bir_lowering=False, debug=False)
    ap = lambda *a, **k: nc.dram_tensor(*a, **k).ap()
    x0 = ap("x0", [D, T], F32, kind="ExternalInput")
    W1T = ap("W1T", [D, FF], F32, kind="ExternalInput")
    W2T = ap("W2T", [FF, D], F32, kind="ExternalInput")
    b1_ = ap("b1pp", [128, FC], F32, kind="ExternalInput")
    b2_ = ap("b2pp", [128, DC], F32, kind="ExternalInput")
    Wq = ap("Wq", [D, D], F32, kind="ExternalInput")
    Wk = ap("Wk", [D, D], F32, kind="ExternalInput")
    Wv = ap("Wv", [D, VDIM], F32, kind="ExternalInput")
    Wg = ap("Wg", [D, VDIM], F32, kind="ExternalInput")
    COS = ap("COS", [128, T], F32, kind="ExternalInput")
    SIN = ap("SIN", [128, T], F32, kind="ExternalInput")
    DT_ = ap("DT", [NH, 4, 128, TS], BF16, kind="ExternalInput")
    WIN = ap("WIN", [NH, 128, TS], BF16, kind="ExternalInput")
    KHAT = ap("KHAT", [NH, 128, 4], F32, kind="ExternalInput")
    IDB = ap("IDB", [128, 128], BF16, kind="ExternalInput")
    onesd_ = ap("onesd", [128, 1], BF16, kind="ExternalInput")
    ones1f_ = ap("ones1f", [1, 128], F32, kind="ExternalInput")
    onesNf_ = ap("onesNf", [128, 1], F32, kind="ExternalInput")

    x1_o = ap("x1", [D, T], F32, kind="ExternalOutput")
    t1_o = ap("T1b", [128, DC, FF], BF16, kind="ExternalOutput")
    t2_o = ap("T2b", [128, FC, D], BF16, kind="ExternalOutput")
    oloc_o = ap("o_loc", [VDIM, T], F32, kind="ExternalOutput")
    qtil_o = ap("q_til", [D, T], BF16, kind="ExternalOutput")
    gsil_o = ap("g_sil", [VDIM, T], BF16, kind="ExternalOutput")
    sloc_o = ap("S_loc", [NH, 128, DV], F32, kind="ExternalOutput")
    dbg_xlnj = ap("dbg_xlnj", [128, DC, TS], BF16, kind="ExternalOutput")
    dbg_hb = ap("dbg_hb", [128, FC, TS], BF16, kind="ExternalOutput")

    scrA1 = ap("scrA1", [128, 8], F32, kind="Internal")
    scrA2 = ap("scrA2", [128, 32], F32, kind="Internal")
    scrB1 = ap("scrB1", [1, 2], F32, kind="Internal")
    scrB2 = ap("scrB2", [1, 2], F32, kind="Internal")
    lnscr = ap("lnscr", [2, T], F32, kind="Internal")

    cx = Ctx(nc)
    p = cx.p
    qL, gL, T_, V, SC, G, ST = cx.qL, cx.gL, cx.T_, cx.V, cx.SC, cx.G, cx.ST

    x = cx.sb([128, DC, T])
    consts = dict(
        onesd_b=cx.sb([128, 1], BF16), ones1f=cx.sb([1, 128]),
        onesNf=cx.sb([128, 1]), b1pp=cx.sb([128, FC]), b2pp=cx.sb([128, DC]),
    )
    cx.pR1 = cx.ps([1, TS]); cx.pR2 = cx.ps([1, TS])
    pA = cx.ps([128, TS]); pB = cx.ps([128, TS])
    pC = cx.ps([128, TS]); pD = cx.ps([128, TS])

    with (nc.Block() as block,):
        lx = p.dma("sync", qL, x[:], x0.rearrange("(dc q) t -> q dc t", q=128))
        p.dma("sync", qL, consts["ones1f"][:], ones1f_)
        p.dma("sync", qL, consts["onesNf"][:], onesNf_)
        p.dma("sync", qL, consts["b1pp"][:], b1_)
        p.dma("sync", qL, consts["b2pp"][:], b2_)
        lcb = p.dma("sync", qL, consts["onesd_b"][:], onesd_)

        with ExitStack() as ph:
            tb = dict(wch=cx.sb([128, FF], F32, ph), junk=cx.sb([128, FF], F32, ph),
                      stage=cx.sb([128, FF], BF16, ph), absacc=cx.sb([128, 64], F32, ph),
                      mval=cx.sb([1, 1], F32, ph), thrp=cx.sb([128, 1], F32, ph),
                      thrn=cx.sb([128, 1], F32, ph), thr2=cx.sb([1, 2], F32, ph),
                      flat=cx.sb([1, 128 * 32], F32, ph))
            p.wait("vector", qL, lcb)
            tb["scrA"] = scrA1; tb["scrB"] = scrB1
            emit_ternarize(cx, W1T, t1_o, D, FF, False, tb, consts)
            tb["scrA"] = scrA2; tb["scrB"] = scrB2
            emit_ternarize(cx, W2T, t2_o, FF, D, True, tb, consts)
            cx.barrier()

        with ExitStack() as ph:
            b = ln_bufs(cx, ph)
            b["rstd_b"] = cx.sb([128, T], F32, ph)
            b["mr_b"] = cx.sb([128, T], F32, ph)
            b["lnscr"] = lnscr
            b["xlnj"] = cx.sb([128, DC, TS], BF16, ph)
            b["hb"] = cx.sb([128, FC, TS], BF16, ph)
            b["w1b"] = cx.sb([128, 2, DC, 512], BF16, ph)
            b["w2b"] = cx.sb([128, 2, FC, 128], BF16, ph)
            b["hb2pp"] = cx.sb([128, DC], F32, ph)
            p.wait("scalar", qL, lcb)
            b["sc_hb2pp"] = p.op("scalar", lambda e: e.mul(b["hb2pp"][:, 0:DC], consts["b2pp"][:, 0:DC], 0.5), sem=SC)
            p.wait("vector", qL, lx)
            st_tern = cx.ST.c
            emit_ffn0(cx, x, b, consts, t1_o, t2_o, pA, st_tern,
                      dbg={"xlnj": dbg_xlnj, "hb": dbg_hb})
            p.wait("sync", V, V.c)
            p.dma("sync", ST, x1_o.rearrange("(dc q) t -> q dc t", q=128), x[:])
            cx.barrier()

        with ExitStack() as ph:
            xlnb = cx.sb([128, DC, T], BF16, ph)
            b3 = ln_bufs(cx, ph)
            b3["rstd_b"] = cx.sb([128, T], F32, ph)
            b3["mr_b"] = cx.sb([128, T], F32, ph)
            b3["lnscr"] = lnscr
            emit_ln_stats(cx, x, b3, consts, cx.pR1, cx.pR2)
            for j in range(NT):
                emit_ln_norm_j(cx, x, xlnb, j, b3)
            cx.barrier()

            cosb = cx.sb([128, T], F32, ph)
            sinb = cx.sb([128, T], F32, ph)
            idb = cx.sb([128, 128], BF16, ph)
            wqh = cx.sb([128, DC, 128], BF16, ph)
            wkh = cx.sb([128, DC, 128], BF16, ph)
            wvh = cx.sb([128, DC, DV], BF16, ph)
            wgh = cx.sb([128, DC, DV], BF16, ph)
            dth = cx.sb([128, 4, TS], BF16, ph)
            winh = cx.sb([128, TS], BF16, ph)
            khath = cx.sb([128, 4], F32, ph)
            qf = cx.sb([128, TS], F32, ph)
            qsw = cx.sb([128, TS], F32, ph)
            qrot = cx.sb([128, TS], BF16, ph)
            krot = cx.sb([128, TS], BF16, ph)
            qtb = cx.sb([128, TS], BF16, ph)
            vtb = cx.sb([128, 4, DV], BF16, ph)
            ktm = cx.sb([128, 4, 128], BF16, ph)
            ssb = cx.sb([128, 4, TS], BF16, ph)
            Sst = cx.sb([128, DV], F32, ph)
            Ssb = cx.sb([128, DV], BF16, ph)
            ostg = cx.sb([128, 2, TS], F32, ph)
            gstg = cx.sb([128, 2, TS], BF16, ph)
            pT = cx.ps([128, 128], BF16, ph)
            pS = cx.ps([128, DV], F32, ph)

            p.dma("gpsimd", gL, cosb[:], COS)
            p.dma("gpsimd", gL, sinb[:], SIN)
            lid = p.dma("gpsimd", gL, idb[:], IDB)
            p.wait("tensor", gL, lid)
            p.wait("vector", gL, lid)

            lam = {h: float(2.0 ** (S_H[h] * TS)) for h in range(NH)}
            for h in range(NH):
                p.wait("gpsimd", T_, T_.c)
                p.dma("gpsimd", gL, wqh[:], Wq.rearrange("(dc q) e -> q dc e", q=128)[:, :, h * 128:(h + 1) * 128])
                p.dma("gpsimd", gL, wkh[:], Wk.rearrange("(dc q) e -> q dc e", q=128)[:, :, h * 128:(h + 1) * 128])
                p.dma("gpsimd", gL, wvh[:], Wv.rearrange("(dc q) e -> q dc e", q=128)[:, :, h * DV:(h + 1) * DV])
                lwg = p.dma("gpsimd", gL, wgh[:], Wg.rearrange("(dc q) e -> q dc e", q=128)[:, :, h * DV:(h + 1) * DV])
                p.wait("sync", V, V.c)
                p.dma("sync", qL, dth[:], DT_[h].rearrange("jb p i -> p jb i"))
                p.dma("sync", qL, winh[:], WIN[h])
                lkh = p.dma("sync", qL, khath[:], KHAT[h])
                p.wait("vector", ST, ST.c)
                p.op("vector", lambda e: e.memset(Sst[:], 0.0), sem=V)
                p.wait("tensor", gL, lwg)
                p.wait("tensor", qL, lkh)
                p.wait("vector", qL, lkh)
                p.wait("tensor", V, V.c)
                st_prev = cx.ST.c
                for ci in range(NT):
                    js = ci * TS
                    p.wait("vector", ST, st_prev)
                    p.wait("scalar", ST, st_prev)
                    for (wt, rot) in ((wqh, qrot), (wkh, krot)):
                        for dc in range(DC):
                            p.op("tensor", lambda e, dc=dc, js=js, wt=wt: e.matmul(
                                pA[:, 0:TS], wt[:, dc, :], xlnb[:, dc, js:js + TS],
                                start=(dc == 0), stop=(dc == DC - 1)), sem=(T_ if dc == DC - 1 else None))
                        p.wait("vector", T_, T_.c)
                        p.op("vector", lambda e: e.tensor_copy(qf[:], pA[:, 0:TS]), sem=V)
                        p.wait("vector", V, V.c)
                        p.op("vector", lambda e: e.stream_shuffle(qsw[:], qf[:], SWAP32))
                        p.op("vector", lambda e, js=js: e.tensor_mul(qf[:], qf[:], cosb[:, js:js + TS]))
                        p.op("vector", lambda e, js=js: e.tensor_mul(qsw[:], qsw[:], sinb[:, js:js + TS]))
                        p.op("vector", lambda e, rot=rot: e.tensor_add(rot[:], qf[:], qsw[:]), sem=V)
                        p.wait("tensor", V, V.c)
                    p.op("vector", lambda e: e.tensor_mul(qtb[:], qrot[:], winh[:]), sem=V)
                    for tb_ in range(4):
                        for dc in range(DC):
                            p.op("tensor", lambda e, tb_=tb_, dc=dc, js=js: e.matmul(
                                pS[:, 0:DV], xlnb[:, dc, js + tb_ * 128:js + (tb_ + 1) * 128],
                                wvh[:, dc, :],
                                start=(dc == 0), stop=(dc == DC - 1)), sem=(T_ if dc == DC - 1 else None))
                        p.wait("vector", T_, T_.c)
                        p.op("vector", lambda e, tb_=tb_: e.tensor_copy(vtb[:, tb_, :], pS[:, 0:DV]), sem=V)
                        p.wait("tensor", V, V.c)
                    for jb in range(4):
                        p.op("tensor", lambda e, jb=jb: e.matmul(
                            pB[:, 0:TS], krot[:, jb * 128:(jb + 1) * 128], qrot[:],
                            start=True, stop=True), sem=T_)
                        p.wait("vector", T_, T_.c)
                        p.op("vector", lambda e, jb=jb: e.tensor_mul(
                            ssb[:, jb, :], pB[:, 0:TS], dth[:, jb, :]), sem=V)
                        p.wait("tensor", V, V.c)
                    for jb in range(4):
                        p.op("tensor", lambda e, jb=jb: e.transpose(
                            pT[:], krot[:, jb * 128:(jb + 1) * 128], idb[:]), sem=T_)
                        p.wait("vector", T_, T_.c)
                        p.op("vector", lambda e, jb=jb: e.tensor_scalar(
                            ktm[:, jb, :], pT[:], khath[:, jb:jb + 1], None, ALU.mult), sem=V)
                        p.wait("tensor", V, V.c)
                    p.op("vector", lambda e: e.tensor_copy(Ssb[:], Sst[:]), sem=V)
                    p.wait("tensor", V, V.c)
                    for vb in range(2):
                        for jb in range(4):
                            p.op("tensor", lambda e, vb=vb, jb=jb: e.matmul(
                                pA[:, 0:TS], vtb[:, jb, vb * 128:(vb + 1) * 128], ssb[:, jb, :],
                                start=(jb == 0), stop=False))
                        p.op("tensor", lambda e, vb=vb: e.matmul(
                            pA[:, 0:TS], Ssb[:, vb * 128:(vb + 1) * 128], qtb[:],
                            start=False, stop=True), sem=T_)
                        p.wait("vector", T_, T_.c)
                        p.op("vector", lambda e, vb=vb: e.tensor_copy(ostg[:, vb, :], pA[:, 0:TS]), sem=V)
                        p.wait("tensor", V, V.c)
                    p.wait("sync", V, V.c)
                    for vb in range(2):
                        p.dma("sync", ST, oloc_o[h * DV + vb * 128: h * DV + (vb + 1) * 128, js:js + TS],
                              ostg[:, vb, :])
                    p.dma("sync", ST, qtil_o[h * 128:(h + 1) * 128, js:js + TS], qtb[:])
                    for vb in range(2):
                        for dc in range(DC):
                            p.op("tensor", lambda e, vb=vb, dc=dc, js=js: e.matmul(
                                pB[:, 0:TS], wgh[:, dc, vb * 128:(vb + 1) * 128],
                                xlnb[:, dc, js:js + TS],
                                start=(dc == 0), stop=(dc == DC - 1)), sem=(T_ if dc == DC - 1 else None))
                        p.wait("scalar", T_, T_.c)
                        p.op("scalar", lambda e, vb=vb: e.activation(
                            gstg[:, vb, :], pB[:, 0:TS], AF.Silu), sem=SC)
                        p.wait("tensor", SC, SC.c)
                    p.wait("sync", SC, SC.c)
                    for vb in range(2):
                        p.dma("sync", ST, gsil_o[h * DV + vb * 128: h * DV + (vb + 1) * 128, js:js + TS],
                              gstg[:, vb, :])
                    for jb in range(4):
                        p.op("tensor", lambda e, jb=jb: e.matmul(
                            pS[:, 0:DV], ktm[:, jb, :], vtb[:, jb, :],
                            start=(jb == 0), stop=(jb == 3)), sem=(T_ if jb == 3 else None))
                    p.wait("vector", T_, T_.c)
                    p.op("vector", lambda e, h=h: e.tensor_scalar(Sst[:], Sst[:], lam[h], None, ALU.mult), sem=V)
                    p.wait("vector", V, V.c)
                    p.op("vector", lambda e: e.tensor_add(Sst[:], Sst[:], pS[:, 0:DV]), sem=V)
                    p.wait("tensor", V, V.c)
                    st_prev = cx.ST.c
                p.wait("sync", V, V.c)
                p.dma("sync", ST, sloc_o[h], Sst[:])

        p.wait("sync", ST, ST.c)
        p.emit(block)
    cx.stack.close()
    cx._semstack.close()
    return nc


def emit_moe(cx, x, xlnb, m, consts):
    p = cx.p
    qL, gL, T_, V, SC, G, ST = cx.qL, cx.gL, cx.T_, cx.V, cx.SC, cx.G, cx.ST
    pA, pD, pC, pB1, pB2, pT = m["pA"], m["pD"], m["pC"], m["pB1"], m["pB2"], m["pT"]
    lg = p.dma("gpsimd", gL, m["gateb"][:], m["gateT_dram"].rearrange("(dc q) e -> q dc e", q=128))
    p.wait("tensor", gL, lg)
    for j in range(NT):
        js = j * TS
        for tb in range(4):
            p.wait("tensor", SC, cx.SC.c)
            for dc in range(DC):
                p.op("tensor", lambda e, tb=tb, dc=dc, js=js: e.matmul(
                    pA[:, 0:8], xlnb[:, dc, js + tb * 128:js + (tb + 1) * 128],
                    m["gateb"][:, dc, :],
                    start=(dc == 0), stop=(dc == DC - 1)), sem=(T_ if dc == DC - 1 else None))
            p.wait("scalar", T_, T_.c)
            p.op("scalar", lambda e: e.activation(m["s_tm"][:, 0:8], pA[:, 0:8], AF.Sigmoid), sem=SC)
            p.wait("vector", SC, SC.c)
            p.op("vector", lambda e: e.tensor_add(m["b_tm"][:, 0:8], m["s_tm"][:, 0:8],
                                                  consts["ebias"][:, 0:8]), sem=V)
            p.wait("vector", V, V.c)
            p.op("vector", lambda e: e.reduce_max(m["m1"][:, 0:1], m["b_tm"][:, 0:8], axis=AX.X), sem=V)
            p.wait("vector", V, V.c)
            p.op("vector", lambda e: e.tensor_scalar(m["mk1"][:, 0:8], m["b_tm"][:, 0:8],
                                                     m["m1"][:, 0:1], None, ALU.is_ge), sem=V)
            p.wait("vector", V, V.c)
            p.op("vector", lambda e: e.scalar_tensor_tensor(m["b_tm"][:, 0:8], m["mk1"][:, 0:8],
                                                            -1e30, m["b_tm"][:, 0:8],
                                                            ALU.mult, ALU.add), sem=V)
            p.wait("vector", V, V.c)
            p.op("vector", lambda e: e.reduce_max(m["m1"][:, 0:1], m["b_tm"][:, 0:8], axis=AX.X), sem=V)
            p.wait("vector", V, V.c)
            p.op("vector", lambda e: e.tensor_scalar(m["mk2"][:, 0:8], m["b_tm"][:, 0:8],
                                                     m["m1"][:, 0:1], None, ALU.is_ge), sem=V)
            p.wait("vector", V, V.c)
            p.op("vector", lambda e: e.tensor_add(m["mk1"][:, 0:8], m["mk1"][:, 0:8], m["mk2"][:, 0:8]))
            p.op("vector", lambda e: e.tensor_mul(m["sc_tmb"][:, 0:8], m["mk1"][:, 0:8], m["s_tm"][:, 0:8]), sem=V)
            p.wait("tensor", V, V.c)
            p.op("tensor", lambda e: e.transpose(pT[0:8, 0:128], m["sc_tmb"][:, 0:8], consts["idb"][:]), sem=T_)
            p.wait("vector", T_, T_.c)
            p.op("vector", lambda e, js=js, tb=tb: e.tensor_copy(
                m["sc_fm"][0:8, js + tb * 128:js + (tb + 1) * 128], pT[0:8, 0:128]), sem=V)
            p.wait("tensor", V, V.c)
    p.op("vector", lambda e: e.tensor_mul(m["s2_fm"][0:8, :], m["sc_fm"][0:8, :], m["sc_fm"][0:8, :]), sem=V)
    p.wait("vector", V, V.c)
    p.op("vector", lambda e: e.tensor_scalar(m["s2_fm"][0:8, :], m["s2_fm"][0:8, :], 0.5, None, ALU.mult), sem=V)
    p.wait("sync", V, V.c)
    p.dma("sync", qL, m["sc_scr"][0], m["sc_fm"][0:8, :])
    vroute = p.dma("sync", qL, m["sc_scr"][1], m["s2_fm"][0:8, :])
    lw_prev = None
    for e_ in range(9):
        shared = (e_ == 8)
        if lw_prev is not None:
            p.wait("gpsimd", T_, T_.c)
        if shared:
            p.dma("gpsimd", gL, m["wA"][:], m["sw1T_dram"].rearrange("(dc q) h -> q dc h", q=128))
            p.dma("gpsimd", gL, m["wB"][:], m["sw3T_dram"].rearrange("(dc q) h -> q dc h", q=128))
            lw = p.dma("gpsimd", gL, m["wC"][:], m["sw2T_dram"].rearrange("(hc q) d -> q hc d", q=128))
            p.wait("vector", gL, lw)
            for hc in range(DC):
                p.op("vector", lambda e, hc=hc: e.tensor_scalar(
                    m["wC"][:, hc, :], m["wC"][:, hc, :], 0.5, None, ALU.mult),
                    sem=(V if hc == DC - 1 else None))
            lwv = cx.V.c
            p.wait("tensor", V, lwv)
        else:
            p.dma("gpsimd", gL, m["wA"][:], m["w1T_dram"][e_].rearrange("(dc q) h -> q dc h", q=128))
            p.dma("gpsimd", gL, m["wB"][:], m["w3T_dram"][e_].rearrange("(dc q) h -> q dc h", q=128))
            lw = p.dma("gpsimd", gL, m["wC"][:], m["w2T_dram"][e_].rearrange("(hc q) d -> q hc d", q=128))
            p.wait("tensor", gL, lw)
        lw_prev = lw
        for j in range(NT):
            js = j * TS
            if not shared:
                p.wait("sync", qL, vroute)
                p.wait("sync", V, cx.V.c)
                p.dma("sync", qL, m["in1b"][:], m["sc_scr"][0, e_:e_ + 1, js:js + TS].to_broadcast([128, TS]))
                dq = p.dma("sync", qL, m["in2b"][:], m["sc_scr"][1, e_:e_ + 1, js:js + TS].to_broadcast([128, TS]))
                p.wait("vector", qL, dq)
                for dc in range(DC):
                    p.op("vector", lambda e, dc=dc, js=js: e.tensor_mul(
                        m["xin1"][:, dc, :], xlnb[:, dc, js:js + TS], m["in1b"][:]))
                    p.op("vector", lambda e, dc=dc, js=js: e.tensor_mul(
                        m["xin2"][:, dc, :], xlnb[:, dc, js:js + TS], m["in2b"][:]),
                        sem=(V if dc == DC - 1 else None))
                vxin = cx.V.c
                p.wait("tensor", V, vxin)
            for hc in range(DC):
                p.wait("tensor", V, cx.V.c)
                p.wait("tensor", SC, cx.SC.c)
                for dc in range(DC):
                    if shared:
                        p.op("tensor", lambda e, dc=dc, hc=hc, js=js: e.matmul(
                            pA[:, 0:TS], m["wA"][:, dc, hc * 128:(hc + 1) * 128],
                            xlnb[:, dc, js:js + TS],
                            start=(dc == 0), stop=(dc == DC - 1)))
                    else:
                        p.op("tensor", lambda e, dc=dc, hc=hc: e.matmul(
                            pA[:, 0:TS], m["wA"][:, dc, hc * 128:(hc + 1) * 128],
                            m["xin1"][:, dc, :],
                            start=(dc == 0), stop=(dc == DC - 1)))
                for dc in range(DC):
                    if shared:
                        p.op("tensor", lambda e, dc=dc, hc=hc, js=js: e.matmul(
                            pD[:, 0:TS], m["wB"][:, dc, hc * 128:(hc + 1) * 128],
                            xlnb[:, dc, js:js + TS],
                            start=(dc == 0), stop=(dc == DC - 1)),
                            sem=(T_ if dc == DC - 1 else None))
                    else:
                        p.op("tensor", lambda e, dc=dc, hc=hc: e.matmul(
                            pD[:, 0:TS], m["wB"][:, dc, hc * 128:(hc + 1) * 128],
                            m["xin2"][:, dc, :],
                            start=(dc == 0), stop=(dc == DC - 1)),
                            sem=(T_ if dc == DC - 1 else None))
                p.wait("scalar", T_, T_.c)
                p.op("scalar", lambda e: e.activation(m["sa"][:], pA[:, 0:TS], AF.Silu), sem=SC)
                p.wait("vector", SC, SC.c)
                p.op("vector", lambda e, hc=hc: e.tensor_mul(m["hbm"][:, hc, :], m["sa"][:], pD[:, 0:TS]), sem=V)
            p.wait("tensor", V, cx.V.c)
            for dc in range(DC):
                p.wait("tensor", V, cx.V.c)
                for hc in range(DC):
                    p.op("tensor", lambda e, hc=hc, dc=dc: e.matmul(
                        pC[:, 0:TS], m["wC"][:, hc, dc * 128:(dc + 1) * 128],
                        m["hbm"][:, hc, :],
                        start=(hc == 0), stop=(hc == DC - 1)), sem=(T_ if hc == DC - 1 else None))
                p.wait("vector", T_, T_.c)
                p.op("vector", lambda e, dc=dc, js=js: e.tensor_add(
                    x[:, dc, js:js + TS], x[:, dc, js:js + TS], pC[:, 0:TS]), sem=V)


def moe_bufs(cx, ph):
    return dict(
        gateb=cx.sb([128, DC, 8], BF16, ph),
        s_tm=cx.sb([128, 8], F32, ph), b_tm=cx.sb([128, 8], F32, ph),
        m1=cx.sb([128, 1], F32, ph), mk1=cx.sb([128, 8], F32, ph), mk2=cx.sb([128, 8], F32, ph),
        sc_tmb=cx.sb([128, 8], BF16, ph),
        sc_fm=cx.sb([8, T], F32, ph), s2_fm=cx.sb([8, T], F32, ph),
        in1b=cx.sb([128, TS], F32, ph), in2b=cx.sb([128, TS], F32, ph),
        xin1=cx.sb([128, DC, TS], BF16, ph), xin2=cx.sb([128, DC, TS], BF16, ph),
        wA=cx.sb([128, DC, 1024], BF16, ph), wB=cx.sb([128, DC, 1024], BF16, ph),
        wC=cx.sb([128, DC, 1024], BF16, ph),
        sa=cx.sb([128, TS], BF16, ph), hbm=cx.sb([128, DC, TS], BF16, ph),
    )


def build_L2():
    nc = bass.Bass("TRN2", target_bir_lowering=False, debug=False)
    ap = lambda *a, **k: nc.dram_tensor(*a, **k).ap()
    x1_i = ap("x1", [D, T], F32, kind="ExternalInput")
    oloc_i = ap("o_loc", [VDIM, T], F32, kind="ExternalInput")
    qtil_i = ap("q_til", [D, T], BF16, kind="ExternalInput")
    gsil_i = ap("g_sil", [VDIM, T], BF16, kind="ExternalInput")
    sall_i = ap("S_all", [NC, NH, 128, DV], F32, kind="ExternalInput")
    wst_i = ap("WST", [NH, 128, NC], F32, kind="ExternalInput")
    Wo_i = ap("Wo", [VDIM, D], F32, kind="ExternalInput")
    gnw_i = ap("gnw2", [128, 2], F32, kind="ExternalInput")
    t1_i = ap("T1b", [128, DC, FF], BF16, kind="ExternalInput")
    t2_i = ap("T2b", [128, FC, D], BF16, kind="ExternalInput")
    b1_ = ap("b1pp", [128, FC], F32, kind="ExternalInput")
    b2_ = ap("b2pp", [128, DC], F32, kind="ExternalInput")
    gateT_i = ap("gateT", [D, 8], F32, kind="ExternalInput")
    ebias_i = ap("ebias", [128, 8], F32, kind="ExternalInput")
    w1T_i = ap("w1T", [8, D, D], F32, kind="ExternalInput")
    w2T_i = ap("w2T", [8, D, D], F32, kind="ExternalInput")
    w3T_i = ap("w3T", [8, D, D], F32, kind="ExternalInput")
    sw1T_i = ap("sw1T", [D, D], F32, kind="ExternalInput")
    sw2T_i = ap("sw2T", [D, D], F32, kind="ExternalInput")
    sw3T_i = ap("sw3T", [D, D], F32, kind="ExternalInput")
    Wk1T_i = ap("Wk1T", [D, D], F32, kind="ExternalInput")
    Wq1T_i = ap("Wq1T", [D, D], F32, kind="ExternalInput")
    Wk2T_i = ap("Wk2T", [D, D], F32, kind="ExternalInput")
    bk1_ = ap("bk1pp", [128, DC], F32, kind="ExternalInput")
    bq1_ = ap("bq1pp", [128, DC], F32, kind="ExternalInput")
    onesd_ = ap("onesd", [128, 1], BF16, kind="ExternalInput")
    onesv_ = ap("onesv", [128, 1], BF16, kind="ExternalInput")
    ones1b_ = ap("ones1b", [1, 128], BF16, kind="ExternalInput")
    idb_ = ap("IDB", [128, 128], BF16, kind="ExternalInput")

    x4_o = ap("x4", [D, T], F32, kind="ExternalOutput")
    qf_o = ap("Qf", [D, T], F32, kind="ExternalOutput")
    zb_o = ap("Zb", [D, T], BF16, kind="ExternalOutput")
    kcum_o = ap("Kcum", [D, T], F32, kind="ExternalOutput")
    scum_o = ap("Scum", [D, T], F32, kind="ExternalOutput")
    ln5_o = ap("ln5", [2, T], F32, kind="ExternalOutput")
    payl_o = ap("payl", [128, 4, 8], F32, kind="ExternalOutput")

    lnscr = ap("lnscr", [2, T], F32, kind="Internal")
    scscr = ap("scscr", [2, 8, T], F32, kind="Internal")

    cx = Ctx(nc)
    p = cx.p
    qL, gL, T_, V, SC, G, ST = cx.qL, cx.gL, cx.T_, cx.V, cx.SC, cx.G, cx.ST

    x = cx.sb([128, DC, T])
    consts = dict(
        onesd_b=cx.sb([128, 1], BF16), onesv_b=cx.sb([128, 1], BF16),
        ones1b=cx.sb([1, 128], BF16), idb=cx.sb([128, 128], BF16),
        b1pp=cx.sb([128, FC]), b2pp=cx.sb([128, DC]),
        ebias=cx.sb([128, 8]), bk1pp=cx.sb([128, DC]), bq1pp=cx.sb([128, DC]),
    )
    cx.pR1 = cx.ps([1, TS]); cx.pR2 = cx.ps([1, TS])
    pA = cx.ps([128, TS]); pB = cx.ps([128, TS])
    pC = cx.ps([128, TS]); pD = cx.ps([128, TS])

    with (nc.Block() as block,):
        lx = p.dma("sync", qL, x[:], x1_i.rearrange("(dc q) t -> q dc t", q=128))
        p.dma("sync", qL, consts["onesd_b"][:], onesd_)
        p.dma("sync", qL, consts["onesv_b"][:], onesv_)
        p.dma("sync", qL, consts["ones1b"][:], ones1b_)
        p.dma("sync", qL, consts["b1pp"][:], b1_)
        p.dma("sync", qL, consts["b2pp"][:], b2_)
        p.dma("sync", qL, consts["ebias"][:], ebias_i)
        p.dma("sync", qL, consts["bk1pp"][:], bk1_)
        p.dma("sync", qL, consts["bq1pp"][:], bq1_)
        lcb = p.dma("sync", qL, consts["idb"][:], idb_)
        p.wait("vector", qL, lcb)

        with ExitStack() as ph:
            sallb = cx.sb([128, NC, DV], F32, ph)
            Sin = cx.sb([128, NH, DV], F32, ph)
            wstc = cx.sb([128, NC], F32, ph)
            SinSb = cx.sb([128, DV], BF16, ph)
            qt_t = cx.sb([128, TS], BF16, ph)
            olc = cx.sb([128, 2, TS], F32, ph)
            gsl = cx.sb([128, 2, TS], BF16, ph)
            ostg = cx.sb([128, 2, TS], F32, ph)
            o2b = cx.sb([128, TS], BF16, ph)
            grow = cx.sb([1, TS], F32, ph)
            growb = cx.sb([1, TS], BF16, ph)
            ogall = cx.sb([128, 16, TS], BF16, ph)
            wob = cx.sb([128, 16, D], BF16, ph)
            gnwc = cx.sb([128, 2], F32, ph)

            for h in range(NH):
                lv = p.dma("sync", qL, sallb[:], sall_i[:, h].rearrange("c p v -> p c v"))
                lw = p.dma("sync", qL, wstc[:], wst_i[h])
                p.wait("vector", qL, lw)
                p.op("vector", lambda e, h=h: e.memset(Sin[:, h, :], 0.0), sem=V)
                p.wait("vector", V, V.c)
                for c_ in range(NC):
                    p.op("vector", lambda e, h=h, c_=c_: e.scalar_tensor_tensor(
                        Sin[:, h, :], sallb[:, c_, :], wstc[:, c_:c_ + 1], Sin[:, h, :],
                        ALU.mult, ALU.add), sem=V)
                    p.wait("vector", V, V.c)
                p.wait("sync", V, V.c)
            lwo = p.dma("gpsimd", gL, wob[:], Wo_i.rearrange("(vc q) d -> q vc d", q=128))
            lgn = p.dma("sync", qL, gnwc[:], gnw_i)
            p.wait("vector", gL, lwo)
            p.wait("vector", qL, lgn)
            for vc in range(16):
                p.op("vector", lambda e, vc=vc: e.tensor_scalar(
                    wob[:, vc, :], wob[:, vc, :], gnwc[:, (vc % 2):(vc % 2) + 1], None, ALU.mult),
                    sem=(V if vc == 15 else None))
            vwo = cx.V.c
            p.wait("tensor", V, vwo)

            lam512 = {(h, ci): float(2.0 ** (S_H[h] * TS * ci)) for h in range(NH) for ci in range(NT)}
            for ci in range(NT):
                js = ci * TS
                for h in range(NH):
                    p.wait("vector", T_, T_.c)
                    p.op("vector", lambda e, h=h, ci=ci: e.tensor_scalar(
                        SinSb[:], Sin[:, h, :], lam512[(h, ci)], None, ALU.mult), sem=V)
                    p.wait("sync", V, V.c)
                    p.dma("sync", qL, qt_t[:], qtil_i[h * 128:(h + 1) * 128, js:js + TS])
                    p.dma("sync", qL, olc[:, 0, :], oloc_i[h * DV:h * DV + 128, js:js + TS])
                    p.dma("sync", qL, olc[:, 1, :], oloc_i[h * DV + 128:h * DV + DV, js:js + TS])
                    p.dma("sync", qL, gsl[:, 0, :], gsil_i[h * DV:h * DV + 128, js:js + TS])
                    lvq = p.dma("sync", qL, gsl[:, 1, :], gsil_i[h * DV + 128:h * DV + DV, js:js + TS])
                    p.wait("tensor", qL, lvq)
                    p.wait("tensor", V, V.c)
                    for vb in range(2):
                        ppp = pA if vb == 0 else pB
                        p.op("tensor", lambda e, vb=vb, ppp=ppp: e.matmul(
                            ppp[:, 0:TS], SinSb[:, vb * 128:(vb + 1) * 128], qt_t[:],
                            start=True, stop=True), sem=(T_ if vb == 1 else None))
                    p.wait("vector", T_, T_.c)
                    for vb in range(2):
                        ppp = pA if vb == 0 else pB
                        p.op("vector", lambda e, vb=vb, ppp=ppp: e.tensor_add(
                            ostg[:, vb, :], olc[:, vb, :], ppp[:, 0:TS]),
                            sem=(V if vb == 1 else None))
                    vo = cx.V.c
                    p.wait("scalar", V, vo)
                    p.wait("scalar", T_, T_.c)
                    for vb in range(2):
                        p.op("scalar", lambda e, vb=vb: e.activation(o2b[:], ostg[:, vb, :], AF.Square), sem=SC)
                        p.wait("tensor", SC, SC.c)
                        p.op("tensor", lambda e, vb=vb: e.matmul(
                            cx.pR1[0:1, 0:TS], consts["onesv_b"][:, 0:1], o2b[:],
                            start=(vb == 0), stop=(vb == 1)), sem=T_)
                        p.wait("scalar", T_, T_.c)
                    p.wait("vector", T_, T_.c)
                    p.op("vector", lambda e: e.tensor_scalar(grow[0:1, :], cx.pR1[0:1, 0:TS],
                                                             EPS_LN, None, ALU.add), sem=V)
                    p.wait("scalar", V, V.c)
                    p.op("scalar", lambda e: e.activation(grow[0:1, :], grow[0:1, :], AF.Ln), sem=SC)
                    p.wait("scalar", SC, SC.c)
                    p.op("scalar", lambda e: e.activation(growb[0:1, :], grow[0:1, :], AF.Exp, scale=-0.5), sem=SC)
                    p.wait("tensor", SC, SC.c)
                    p.op("tensor", lambda e: e.matmul(pC[:, 0:TS], consts["ones1b"][0:1, :],
                                                      growb[0:1, :], start=True, stop=True), sem=T_)
                    p.wait("vector", T_, T_.c)
                    for vb in range(2):
                        p.op("vector", lambda e, vb=vb: e.tensor_mul(
                            ostg[:, vb, :], ostg[:, vb, :], pC[:, 0:TS]))
                        p.op("vector", lambda e, vb=vb, h=h: e.tensor_mul(
                            ogall[:, h * 2 + vb, :], ostg[:, vb, :], gsl[:, vb, :]),
                            sem=(V if vb == 1 else None))
                    p.wait("tensor", V, V.c)
                p.wait("tensor", V, V.c)
                for dc in range(DC):
                    p.wait("tensor", V, V.c)
                    for vc in range(16):
                        p.op("tensor", lambda e, vc=vc, dc=dc: e.matmul(
                            pD[:, 0:TS], wob[:, vc, dc * 128:(dc + 1) * 128], ogall[:, vc, :],
                            start=(vc == 0), stop=(vc == 15)), sem=(T_ if vc == 15 else None))
                    p.wait("vector", T_, T_.c)
                    p.op("vector", lambda e, dc=dc, js=js: e.tensor_add(
                        x[:, dc, js:js + TS], x[:, dc, js:js + TS], pD[:, 0:TS]), sem=V)
            cx.barrier()

        with ExitStack() as ph:
            bB = ln_bufs(cx, ph)
            bB["rstd_b"] = cx.sb([128, T], F32, ph)
            bB["mr_b"] = cx.sb([128, T], F32, ph)
            bB["lnscr"] = lnscr
            bB["xlnj"] = cx.sb([128, DC, TS], BF16, ph)
            bB["hb"] = cx.sb([128, FC, TS], BF16, ph)
            bB["w1b"] = cx.sb([128, 2, DC, 512], BF16, ph)
            bB["w2b"] = cx.sb([128, 2, FC, 128], BF16, ph)
            bB["hb2pp"] = cx.sb([128, DC], F32, ph)
            p.wait("scalar", qL, lcb)
            bB["sc_hb2pp"] = p.op("scalar", lambda e: e.mul(bB["hb2pp"][:, 0:DC], consts["b2pp"][:, 0:DC], 0.5), sem=SC)
            emit_ffn0(cx, x, bB, consts, t1_i, t2_i, pA, None)
            cx.barrier()

        with ExitStack() as ph:
            xlnb = cx.sb([128, DC, T], BF16, ph)
            with ExitStack() as lnph:
                bC = ln_bufs(cx, lnph)
                bC["rstd_b"] = cx.sb([128, T], F32, lnph)
                bC["mr_b"] = cx.sb([128, T], F32, lnph)
                bC["lnscr"] = lnscr
                emit_ln_stats(cx, x, bC, consts, cx.pR1, cx.pR2)
                for j in range(NT):
                    emit_ln_norm_j(cx, x, xlnb, j, bC)
                cx.barrier()
            m = moe_bufs(cx, ph)
            m.update(pA=pA, pD=pD, pC=pC, pB1=pB, pB2=pB,
                     gateT_dram=gateT_i, w1T_dram=w1T_i, w2T_dram=w2T_i, w3T_dram=w3T_i,
                     sw1T_dram=sw1T_i, sw2T_dram=sw2T_i, sw3T_dram=sw3T_i,
                     sc_scr=scscr)
            pTx = cx.ps([128, 128], BF16, ph)
            m["pT"] = pTx
            emit_moe(cx, x, xlnb, m, consts)
            p.wait("sync", V, V.c)
            p.dma("sync", ST, x4_o.rearrange("(dc q) t -> q dc t", q=128), x[:])
            cx.barrier()

        with ExitStack() as ph:
            xlnb = cx.sb([128, DC, T], BF16, ph)
            bD = dict(rstd_b=cx.sb([128, T], F32, ph), mr_b=cx.sb([128, T], F32, ph),
                     lnscr=lnscr)
            with ExitStack() as lnph:
                b2x = ln_bufs(cx, lnph)
                b2x.update(rstd_b=bD["rstd_b"], mr_b=bD["mr_b"], lnscr=lnscr)
                emit_ln_stats(cx, x, b2x, consts, cx.pR1, cx.pR2)
                bD["bcast_ql"] = b2x["bcast_ql"]
                for j in range(NT):
                    emit_ln_norm_j(cx, x, xlnb, j, b2x)
                p.dma("sync", ST, ln5_o[0:1, :], b2x["srow"][0:1, :])
                p.dma("sync", ST, ln5_o[1:2, :], b2x["mrow"][0:1, :])
                cx.barrier()

            wkb = cx.sb([128, DC, 128], BF16, ph)
            wqb = cx.sb([128, DC, 128], BF16, ph)
            wzb = cx.sb([128, DC, 128], BF16, ph)
            Kf = cx.sb([128, T], F32, ph)
            Vf = cx.sb([128, T], F32, ph)
            Qf = cx.sb([128, T], F32, ph)
            Zf = cx.sb([128, T], F32, ph)
            KCf = cx.sb([128, T], F32, ph)
            SCf = cx.sb([128, T], F32, ph)
            onescol = cx.sb([128, 1], F32, ph)
            paysb = cx.sb([128, 4, 8], F32, ph)
            p.op("vector", lambda e: e.memset(onescol[:], 1.0), sem=V)
            for ec in range(DC):
                p.wait("gpsimd", T_, T_.c)
                p.dma("gpsimd", gL, wkb[:], Wk1T_i.rearrange("(dc q) e -> q dc e", q=128)[:, :, ec * 128:(ec + 1) * 128])
                p.dma("gpsimd", gL, wqb[:], Wq1T_i.rearrange("(dc q) e -> q dc e", q=128)[:, :, ec * 128:(ec + 1) * 128])
                lw = p.dma("gpsimd", gL, wzb[:], Wk2T_i.rearrange("(dc q) e -> q dc e", q=128)[:, :, ec * 128:(ec + 1) * 128])
                p.wait("tensor", gL, lw)
                p.wait("tensor", G, G.c)
                p.wait("tensor", ST, ST.c)
                p.wait("scalar", G, G.c)
                p.wait("scalar", V, V.c)
                p.wait("scalar", ST, ST.c)
                for (wt, dstf, actf, biasc) in (
                    (wkb, Kf, AF.Exp, consts["bk1pp"]),
                    (wqb, Qf, AF.Sigmoid, consts["bq1pp"]),
                    (wzb, Zf, AF.Identity, None),
                ):
                    for j in range(NT):
                        js = j * TS
                        for dc in range(DC):
                            p.op("tensor", lambda e, dc=dc, js=js, wt=wt: e.matmul(
                                pA[:, 0:TS], wt[:, dc, :], xlnb[:, dc, js:js + TS],
                                start=(dc == 0), stop=(dc == DC - 1)), sem=(T_ if dc == DC - 1 else None))
                        p.wait("scalar", T_, T_.c)
                        if biasc is not None:
                            p.op("scalar", lambda e, dstf=dstf, js=js, actf=actf, biasc=biasc, ec=ec: e.activation(
                                dstf[0:128, js:js + TS], pA[:, 0:TS], actf,
                                bias=biasc[:, ec:ec + 1]), sem=SC)
                        else:
                            p.op("scalar", lambda e, dstf=dstf, js=js: e.copy(
                                dstf[0:128, js:js + TS], pA[:, 0:TS]), sem=SC)
                        p.wait("tensor", SC, SC.c)
                p.wait("vector", SC, SC.c)
                p.op("vector", lambda e, ec=ec: e.tensor_mul(Vf[:], x[:, ec, :], bD["rstd_b"][:]))
                p.op("vector", lambda e: e.tensor_sub(Vf[:], Vf[:], bD["mr_b"][:]), sem=V)
                p.wait("vector", V, V.c)
                p.op("vector", lambda e: e.tensor_mul(Vf[:], Kf[:], Vf[:]), sem=V)
                p.op("vector", lambda e: e.tensor_tensor_scan(
                    KCf[:], onescol[:].to_broadcast([128, T]), Kf[:], 0.0, ALU.mult, ALU.add), sem=G)
                p.op("vector", lambda e: e.tensor_tensor_scan(
                    SCf[:], onescol[:].to_broadcast([128, T]), Vf[:], 0.0, ALU.mult, ALU.add), sem=G)
                p.wait("vector", G, G.c)
                p.op("vector", lambda e, ec=ec: e.tensor_copy(paysb[:, 0, ec:ec + 1], KCf[:, T - 1:T]))
                p.op("vector", lambda e, ec=ec: e.tensor_copy(paysb[:, 1, ec:ec + 1], SCf[:, T - 1:T]))
                p.op("vector", lambda e, ec=ec: e.tensor_copy(paysb[:, 2, ec:ec + 1], Qf[:, T - 1:T]))
                p.op("vector", lambda e, ec=ec: e.tensor_copy(paysb[:, 3, ec:ec + 1], Zf[:, T - 1:T]), sem=V)
                p.wait("sync", V, V.c)
                p.wait("sync", G, G.c)
                p.wait("sync", SC, SC.c)
                p.dma("sync", ST, qf_o[ec * 128:(ec + 1) * 128, :], Qf[:])
                p.dma("sync", ST, kcum_o[ec * 128:(ec + 1) * 128, :], KCf[:])
                p.dma("sync", ST, scum_o[ec * 128:(ec + 1) * 128, :], SCf[:])
                p.wait("gpsimd", SC, SC.c)
                p.dma("gpsimd", ST, zb_o[ec * 128:(ec + 1) * 128, :], Zf[:])
            p.wait("sync", V, V.c)
            p.dma("sync", ST, payl_o, paysb[:])

        p.wait("sync", ST, ST.c)
        p.wait("gpsimd", ST, ST.c)
        p.emit(block)
    cx.stack.close()
    cx._semstack.close()
    return nc


K2SC = float(0.02 / np.sqrt(D))


def build_L3():
    nc = bass.Bass("TRN2", target_bir_lowering=False, debug=False)
    ap = lambda *a, **k: nc.dram_tensor(*a, **k).ap()
    x4_i = ap("x4", [D, T], F32, kind="ExternalInput")
    qf_i = ap("Qf", [D, T], F32, kind="ExternalInput")
    zb_i = ap("Zb", [D, T], BF16, kind="ExternalInput")
    kcum_i = ap("Kcum", [D, T], F32, kind="ExternalInput")
    scum_i = ap("Scum", [D, T], F32, kind="ExternalInput")
    ln5_i = ap("ln5", [2, T], F32, kind="ExternalInput")
    payall_i = ap("payall", [NC, 128, 4, 8], F32, kind="ExternalInput")
    prevp_i = ap("prevp", [128, 4, 8], F32, kind="ExternalInput")
    maskc_i = ap("maskc", [128, NC], F32, kind="ExternalInput")
    wpT_i = ap("WprojT", [D, D], F32, kind="ExternalInput")
    bpj_i = ap("bprojpp", [128, DC], F32, kind="ExternalInput")
    bk2_i = ap("bk2pp", [128, DC], F32, kind="ExternalInput")

    x5_o = ap("x5p", [D, T], F32, kind="ExternalOutput")
    q2_o = ap("Q2sh", [D, T], BF16, kind="ExternalOutput")
    t3_o = ap("t3", [128, 8], F32, kind="ExternalOutput")

    cx = Ctx(nc)
    p = cx.p
    qL, gL, T_, V, SC, G, ST = cx.qL, cx.gL, cx.T_, cx.V, cx.SC, cx.G, cx.ST

    x = cx.sb([128, DC, T])
    pA = cx.ps([128, TS]); pB = cx.ps([128, TS])
    rstd_b = cx.sb([128, T]); mr_b = cx.sb([128, T])
    paysbA = cx.sb([128, NC, 32])
    prevp = cx.sb([128, 4, 8])
    maskc = cx.sb([128, NC])
    pref = cx.sb([128, 32])
    obcol = cx.sb([128, 8])
    q2bcol = cx.sb([128, 8])
    tmpc = cx.sb([128, 8])
    bk2s = cx.sb([128, DC])
    bpjc = cx.sb([128, DC])
    t3sb = cx.sb([128, 8])
    onescol = cx.sb([128, 1])
    Kg = cx.sb([128, T]); Rr = cx.sb([128, T]); Sg = cx.sb([128, T])
    Qb = cx.sb([128, T]); Oar = cx.sb([128, T]); Vf = cx.sb([128, T])
    K2f = cx.sb([128, T]); S2f = cx.sb([128, T])
    zshb = cx.sb([128, T], BF16)
    q2shb = cx.sb([128, T], BF16)
    opb = cx.sb([128, DC, T], BF16)
    wpb = cx.sb([128, DC, 128], BF16)

    with (nc.Block() as block,):
        lx = p.dma("sync", qL, x[:], x4_i.rearrange("(dc q) t -> q dc t", q=128))
        p.dma("sync", qL, rstd_b[:], ln5_i[0:1, :].to_broadcast([128, T]))
        lr = p.dma("sync", qL, mr_b[:], ln5_i[1:2, :].to_broadcast([128, T]))
        p.dma("sync", qL, paysbA[:], payall_i.rearrange("c p f e -> p c (f e)"))
        p.dma("sync", qL, prevp[:], prevp_i)
        p.dma("sync", qL, maskc[:], maskc_i)
        lb = p.dma("sync", qL, bk2s[:], bk2_i)
        lbp = p.dma("sync", qL, bpjc[:], bpj_i)
        p.wait("vector", qL, lbp)
        p.op("vector", lambda e: e.memset(onescol[:], 1.0))
        p.op("vector", lambda e: e.memset(pref[:], 0.0), sem=V)
        p.wait("vector", V, V.c)
        for c_ in range(NC):
            p.op("vector", lambda e, c_=c_: e.scalar_tensor_tensor(
                pref[:], paysbA[:, c_, :], maskc[:, c_:c_ + 1], pref[:],
                ALU.mult, ALU.add), sem=V)
            p.wait("vector", V, V.c)
        p.op("vector", lambda e: e.tensor_scalar(tmpc[:, 0:8], pref[:, 0:8], 1e-6, None, ALU.add), sem=V)
        p.wait("vector", V, V.c)
        p.op("vector", lambda e: e.reciprocal(tmpc[:, 0:8], tmpc[:, 0:8]), sem=V)
        p.wait("vector", V, V.c)
        p.op("vector", lambda e: e.tensor_mul(obcol[:, 0:8], pref[:, 8:16], tmpc[:, 0:8]), sem=V)
        p.wait("vector", V, V.c)
        p.op("vector", lambda e: e.tensor_mul(obcol[:, 0:8], obcol[:, 0:8], prevp[:, 2, :]), sem=V)
        p.wait("vector", V, V.c)
        p.op("vector", lambda e: e.tensor_scalar(q2bcol[:, 0:8], prevp[:, 2, :], K2SC, None, ALU.mult), sem=V)
        p.wait("vector", V, V.c)

        for ec in range(DC):
            p.wait("sync", V, V.c)
            p.wait("sync", ST, ST.c)
            p.dma("sync", qL, Kg[:], kcum_i[ec * 128:(ec + 1) * 128, :])
            p.dma("sync", qL, Sg[:], scum_i[ec * 128:(ec + 1) * 128, :])
            p.dma("sync", qL, Qb[:], qf_i[ec * 128:(ec + 1) * 128, :])
            lvz = p.dma("sync", qL, zshb[:, 1:T], zb_i[ec * 128:(ec + 1) * 128, 0:T - 1])
            p.wait("vector", qL, lvz)
            p.op("vector", lambda e, ec=ec: e.tensor_scalar(
                Kg[:], Kg[:], pref[:, ec:ec + 1], 1e-6, ALU.add, op1=ALU.add), sem=V)
            p.wait("vector", V, V.c)
            p.op("vector", lambda e: e.reciprocal(Rr[:], Kg[:]), sem=V)
            p.op("vector", lambda e, ec=ec: e.tensor_scalar(
                Sg[:], Sg[:], pref[:, 8 + ec:9 + ec], None, ALU.add), sem=V)
            p.wait("vector", V, V.c)
            p.op("vector", lambda e: e.tensor_mul(Sg[:], Sg[:], Rr[:]), sem=V)
            p.wait("vector", V, V.c)
            p.op("vector", lambda e: e.tensor_mul(Oar[:], Sg[:], Qb[:]), sem=V)
            p.wait("vector", V, V.c)
            p.op("vector", lambda e, ec=ec: e.tensor_mul(Vf[:], x[:, ec, :], rstd_b[:]))
            p.op("vector", lambda e: e.tensor_sub(Vf[:], Vf[:], mr_b[:]), sem=V)
            p.wait("vector", V, V.c)
            p.op("vector", lambda e: e.tensor_sub(Vf[:, 1:T], Vf[:, 1:T], Oar[:, 0:T - 1]))
            p.op("vector", lambda e, ec=ec: e.tensor_sub(Vf[:, 0:1], Vf[:, 0:1], obcol[:, ec:ec + 1]), sem=V)
            p.wait("vector", V, V.c)
            p.op("vector", lambda e: e.tensor_mul(Vf[:, 0:1], Vf[:, 0:1], maskc[:, 0:1]))
            p.op("vector", lambda e, ec=ec: e.tensor_copy(zshb[:, 0:1], prevp[:, 3, ec:ec + 1]), sem=V)
            p.wait("scalar", V, V.c)
            p.op("scalar", lambda e, ec=ec: e.activation(
                K2f[:], zshb[:], AF.Sigmoid, bias=bk2s[:, ec:ec + 1], scale=K2SC), sem=SC)
            p.wait("vector", SC, SC.c)
            p.op("vector", lambda e: e.tensor_mul(K2f[:], K2f[:], Vf[:]), sem=V)
            p.wait("vector", V, V.c)
            p.op("vector", lambda e: e.tensor_tensor_scan(
                S2f[:], onescol[:, 0:1].to_broadcast([128, T]), K2f[:], 0.0,
                ALU.mult, ALU.add), sem=V)
            p.wait("vector", V, V.c)
            p.op("vector", lambda e, ec=ec: e.tensor_copy(t3sb[:, ec:ec + 1], S2f[:, T - 1:T]), sem=V)
            p.op("vector", lambda e: e.tensor_scalar(q2shb[:, 1:T], Qb[:, 0:T - 1], K2SC, None, ALU.mult))
            p.op("vector", lambda e, ec=ec: e.tensor_copy(q2shb[:, 0:1], q2bcol[:, ec:ec + 1]), sem=V)
            p.wait("vector", V, V.c)
            p.wait("sync", V, V.c)
            p.dma("sync", ST, q2_o[ec * 128:(ec + 1) * 128, :], q2shb[:])
            p.op("vector", lambda e: e.tensor_mul(S2f[:], S2f[:], q2shb[:]), sem=V)
            p.wait("vector", V, V.c)
            p.op("vector", lambda e, ec=ec: e.tensor_add(opb[:, ec, :], Oar[:], S2f[:]), sem=V)
        p.wait("tensor", V, V.c)
        for dc in range(DC):
            p.wait("gpsimd", T_, T_.c)
            lw = p.dma("gpsimd", gL, wpb[:], wpT_i.rearrange("(ec q) d -> q ec d", q=128)[:, :, dc * 128:(dc + 1) * 128])
            p.wait("tensor", gL, lw)
            for j in range(NT):
                js = j * TS
                p.wait("tensor", V, V.c)
                for ec in range(DC):
                    p.op("tensor", lambda e, ec=ec, js=js: e.matmul(
                        pA[:, 0:TS], wpb[:, ec, :], opb[:, ec, js:js + TS],
                        start=(ec == 0), stop=(ec == DC - 1)), sem=(T_ if ec == DC - 1 else None))
                p.wait("vector", T_, T_.c)
                p.op("vector", lambda e, dc=dc, js=js: e.scalar_tensor_tensor(
                    x[:, dc, js:js + TS], pA[:, 0:TS], bpjc[:, dc:dc + 1],
                    x[:, dc, js:js + TS], ALU.add, ALU.add), sem=V)
        p.wait("sync", V, V.c)
        p.dma("sync", ST, x5_o.rearrange("(dc q) t -> q dc t", q=128), x[:])
        p.dma("sync", ST, t3_o, t3sb[:])
        p.wait("sync", ST, ST.c)
        p.emit(block)
    cx.stack.close()
    cx._semstack.close()
    return nc


def build_L4():
    nc = bass.Bass("TRN2", target_bir_lowering=False, debug=False)
    ap = lambda *a, **k: nc.dram_tensor(*a, **k).ap()
    x5_i = ap("x5p", [D, T], F32, kind="ExternalInput")
    q2_i = ap("Q2sh", [D, T], BF16, kind="ExternalInput")
    t3all_i = ap("t3all", [NC, 128, 8], F32, kind="ExternalInput")
    maskc_i = ap("maskc", [128, NC], F32, kind="ExternalInput")
    wpT_i = ap("WprojT", [D, D], F32, kind="ExternalInput")
    gateT_i = ap("gateT", [D, 8], F32, kind="ExternalInput")
    ebias_i = ap("ebias", [128, 8], F32, kind="ExternalInput")
    w1T_i = ap("w1T", [8, D, D], F32, kind="ExternalInput")
    w2T_i = ap("w2T", [8, D, D], F32, kind="ExternalInput")
    w3T_i = ap("w3T", [8, D, D], F32, kind="ExternalInput")
    sw1T_i = ap("sw1T", [D, D], F32, kind="ExternalInput")
    sw2T_i = ap("sw2T", [D, D], F32, kind="ExternalInput")
    sw3T_i = ap("sw3T", [D, D], F32, kind="ExternalInput")
    onesd_ = ap("onesd", [128, 1], BF16, kind="ExternalInput")
    idb_ = ap("IDB", [128, 128], BF16, kind="ExternalInput")

    y_o = ap("y", [T, D], F32, kind="ExternalOutput")

    lnscr = ap("lnscr", [2, T], F32, kind="Internal")
    scscr = ap("scscr", [2, 8, T], F32, kind="Internal")

    cx = Ctx(nc)
    p = cx.p
    qL, gL, T_, V, SC, G, ST = cx.qL, cx.gL, cx.T_, cx.V, cx.SC, cx.G, cx.ST

    x = cx.sb([128, DC, T])
    consts = dict(
        onesd_b=cx.sb([128, 1], BF16), idb=cx.sb([128, 128], BF16),
        ebias=cx.sb([128, 8]),
    )
    cx.pR1 = cx.ps([1, TS]); cx.pR2 = cx.ps([1, TS])
    pA = cx.ps([128, TS]); pB = cx.ps([128, TS])
    pC = cx.ps([128, TS]); pD = cx.ps([128, TS])

    with (nc.Block() as block,):
        lx = p.dma("sync", qL, x[:], x5_i.rearrange("(dc q) t -> q dc t", q=128))
        p.dma("sync", qL, consts["onesd_b"][:], onesd_)
        p.dma("sync", qL, consts["ebias"][:], ebias_i)
        lcb = p.dma("sync", qL, consts["idb"][:], idb_)
        p.wait("vector", qL, lcb)

        with ExitStack() as ph:
            t3a = cx.sb([128, NC, 8], F32, ph)
            maskc = cx.sb([128, NC], F32, ph)
            pref2 = cx.sb([128, 8], F32, ph)
            wpb2 = cx.sb([128, DC, D], BF16, ph)
            q2sb = cx.sb([128, DC, T], BF16, ph)
            p.dma("sync", qL, t3a[:], t3all_i.rearrange("c p e -> p c e"))
            lm = p.dma("sync", qL, maskc[:], maskc_i)
            p.dma("gpsimd", gL, wpb2[:], wpT_i.rearrange("(ec q) d -> q ec d", q=128))
            lq = p.dma("gpsimd", gL, q2sb[:], q2_i.rearrange("(ec q) t -> q ec t", q=128))
            p.wait("vector", qL, lm)
            p.op("vector", lambda e: e.memset(pref2[:], 0.0), sem=V)
            p.wait("vector", V, V.c)
            for c_ in range(NC):
                p.op("vector", lambda e, c_=c_: e.scalar_tensor_tensor(
                    pref2[:], t3a[:, c_, :], maskc[:, c_:c_ + 1], pref2[:],
                    ALU.mult, ALU.add), sem=V)
                p.wait("vector", V, V.c)
            p.wait("vector", gL, gL.c)
            for ec in range(DC):
                p.op("vector", lambda e, ec=ec: e.tensor_scalar(
                    wpb2[:, ec, :], wpb2[:, ec, :], pref2[:, ec:ec + 1], None, ALU.mult),
                    sem=(V if ec == DC - 1 else None))
            p.wait("tensor", V, V.c)
            p.wait("tensor", gL, lq)
            p.wait("vector", qL, lx)
            for dc in range(DC):
                for j in range(NT):
                    js = j * TS
                    p.wait("tensor", V, V.c)
                    for ec in range(DC):
                        p.op("tensor", lambda e, ec=ec, dc=dc, js=js: e.matmul(
                            pA[:, 0:TS], wpb2[:, ec, dc * 128:(dc + 1) * 128],
                            q2sb[:, ec, js:js + TS],
                            start=(ec == 0), stop=(ec == DC - 1)), sem=(T_ if ec == DC - 1 else None))
                    p.wait("vector", T_, T_.c)
                    p.op("vector", lambda e, dc=dc, js=js: e.tensor_add(
                        x[:, dc, js:js + TS], x[:, dc, js:js + TS], pA[:, 0:TS]), sem=V)
            cx.barrier()

        with ExitStack() as ph:
            xlnb = cx.sb([128, DC, T], BF16, ph)
            with ExitStack() as lnph:
                bC = ln_bufs(cx, lnph)
                bC["rstd_b"] = cx.sb([128, T], F32, lnph)
                bC["mr_b"] = cx.sb([128, T], F32, lnph)
                bC["lnscr"] = lnscr
                emit_ln_stats(cx, x, bC, consts, cx.pR1, cx.pR2)
                for j in range(NT):
                    emit_ln_norm_j(cx, x, xlnb, j, bC)
                cx.barrier()
            m = moe_bufs(cx, ph)
            m.update(pA=pA, pD=pD, pC=pC, pB1=pB, pB2=pB,
                     gateT_dram=gateT_i, w1T_dram=w1T_i, w2T_dram=w2T_i, w3T_dram=w3T_i,
                     sw1T_dram=sw1T_i, sw2T_dram=sw2T_i, sw3T_dram=sw3T_i,
                     sc_scr=scscr)
            pTx = cx.ps([128, 128], BF16, ph)
            m["pT"] = pTx
            emit_moe(cx, x, xlnb, m, consts)
            cx.barrier()

        with ExitStack() as ph:
            hib = cx.sb([128, 128], BF16, ph)
            lob = cx.sb([128, 128], BF16, ph)
            stg = cx.sb([128, D], F32, ph)
            pT1 = cx.ps([128, 128], BF16, ph)
            pT2 = cx.ps([128, 128], BF16, ph)
            for tb in range(16):
                ts_ = tb * 128
                p.wait("vector", ST, ST.c)
                p.wait("scalar", ST, ST.c)
                for dc in range(DC):
                    p.wait("vector", T_, T_.c)
                    p.op("vector", lambda e, dc=dc, ts_=ts_: e.tensor_copy(
                        hib[:], x[:, dc, ts_:ts_ + 128]), sem=V)
                    p.wait("vector", V, V.c)
                    p.op("vector", lambda e, dc=dc, ts_=ts_: e.tensor_sub(
                        lob[:], x[:, dc, ts_:ts_ + 128], hib[:]), sem=V)
                    p.wait("tensor", V, V.c)
                    p.op("tensor", lambda e: e.transpose(pT1[:], hib[:], consts["idb"][:]))
                    p.op("tensor", lambda e: e.transpose(pT2[:], lob[:], consts["idb"][:]), sem=T_)
                    p.wait("scalar", T_, T_.c)
                    p.op("scalar", lambda e, dc=dc: e.copy(stg[:, dc * 128:(dc + 1) * 128], pT1[:]), sem=SC)
                    p.wait("vector", SC, SC.c)
                    p.op("vector", lambda e, dc=dc: e.tensor_add(
                        stg[:, dc * 128:(dc + 1) * 128], stg[:, dc * 128:(dc + 1) * 128], pT2[:]), sem=V)
                    p.wait("tensor", V, V.c)
                p.wait("sync", V, V.c)
                p.dma("sync", ST, y_o[ts_:ts_ + 128, :], stg[:])
        p.wait("sync", ST, ST.c)
        p.emit(block)
    cx.stack.close()
    cx._semstack.close()
    return nc


def _rope_tables(c):
    inv = 1.0 / (10000.0 ** (np.arange(0, DQK, 2, dtype=np.float64) / DQK))
    t = np.arange(c * T, (c + 1) * T, dtype=np.float64)[None, :]
    ang = inv[:, None] * t
    cos = np.cos(ang); sin = np.sin(ang)
    Cc = np.empty((128, T), np.float32); Ss = np.empty((128, T), np.float32)
    Cc[0::2] = cos; Cc[1::2] = cos
    Ss[0::2] = -sin; Ss[1::2] = sin
    return Cc, Ss


def _decay_consts():
    DT = np.zeros((NH, 4, 128, TS), np.float64)
    WIN = np.zeros((NH, 128, TS), np.float64)
    KH = np.zeros((NH, 4, 128), np.float64)
    i = np.arange(TS, dtype=np.float64)
    for h in range(NH):
        s = S_H[h]
        for jb in range(4):
            j = jb * 128 + np.arange(128, dtype=np.float64)
            diff = i[None, :] - j[:, None]
            DT[h, jb] = np.where(diff >= 0, np.exp2(diff * s) * SC_QK, 0.0)
            KH[h, jb] = np.exp2(s * (TS - 1 - j))
        WIN[h, :, :] = (np.exp2(s * (i + 1)) * SC_QK)[None, :]
    return DT.astype(BF), WIN.astype(BF), np.ascontiguousarray(KH.transpose(0, 2, 1)).astype(np.float32)


def make_l1_inputs(inp):
    x0 = np.asarray(inp["L0_attn_hidden_states"], np.float32).reshape(SEQ, D)
    x0T = np.ascontiguousarray(x0.T)
    W1T = np.ascontiguousarray(np.asarray(inp["L0_ffn_W1"], np.float32).T)
    W2T = np.ascontiguousarray(np.asarray(inp["L0_ffn_W2"], np.float32).T)
    b1pp = np.ascontiguousarray(np.asarray(inp["L0_ffn_b1"], np.float32).reshape(FC, 128).T)
    b2pp = np.ascontiguousarray(np.asarray(inp["L0_ffn_b2"], np.float32).reshape(DC, 128).T)
    DT, WIN, KH = _decay_consts()
    common = {
        "W1T": W1T, "W2T": W2T, "b1pp": b1pp, "b2pp": b2pp,
        "Wq": np.ascontiguousarray(np.asarray(inp["L0_attn_Wq"], np.float32)),
        "Wk": np.ascontiguousarray(np.asarray(inp["L0_attn_Wk"], np.float32)),
        "Wv": np.ascontiguousarray(np.asarray(inp["L0_attn_Wv"], np.float32)),
        "Wg": np.ascontiguousarray(np.asarray(inp["L0_attn_Wg"], np.float32)),
        "DT": DT, "WIN": WIN, "KHAT": KH,
        "IDB": np.eye(128, dtype=BF),
        "onesd": np.full((128, 1), 1.0 / D, BF),
        "ones1f": np.ones((1, 128), np.float32),
        "onesNf": np.full((128, 1), 1.0 / (D * FF), np.float32),
    }
    in_maps = []
    for c in range(NC):
        Cc, Ss = _rope_tables(c)
        m = dict(common)
        m["x0"] = np.ascontiguousarray(x0T[:, c * T:(c + 1) * T])
        m["COS"] = Cc; m["SIN"] = Ss
        in_maps.append(m)
    return in_maps


def make_l2_inputs(inp, r1res):
    f32 = lambda a: np.ascontiguousarray(np.asarray(a, np.float32))
    S_all = np.stack([r1res[c]["S_loc"] for c in range(NC)]).astype(np.float32)
    common = {
        "S_all": S_all,
        "Wo": f32(inp["L0_attn_Wo"]),
        "gnw2": np.ascontiguousarray(np.asarray(inp["L0_attn_g_norm_w"], np.float32).reshape(2, 128).T),
        "T1b": r1res[0]["T1b"], "T2b": r1res[0]["T2b"],
        "b1pp": np.ascontiguousarray(np.asarray(inp["L0_ffn_b1"], np.float32).reshape(FC, 128).T),
        "b2pp": np.ascontiguousarray(np.asarray(inp["L0_ffn_b2"], np.float32).reshape(DC, 128).T),
        "gateT": np.ascontiguousarray(np.asarray(inp["L1_ffn_gate_w"], np.float32).T),
        "ebias": np.ascontiguousarray(np.broadcast_to(np.asarray(inp["L1_ffn_expert_bias"], np.float32), (128, 8))),
        "w1T": np.ascontiguousarray(np.asarray(inp["L1_ffn_w1"], np.float32).transpose(0, 2, 1)),
        "w2T": np.ascontiguousarray(np.asarray(inp["L1_ffn_w2"], np.float32).transpose(0, 2, 1)),
        "w3T": np.ascontiguousarray(np.asarray(inp["L1_ffn_w3"], np.float32).transpose(0, 2, 1)),
        "sw1T": np.ascontiguousarray(np.asarray(inp["L1_ffn_sw1"], np.float32).T),
        "sw2T": np.ascontiguousarray(np.asarray(inp["L1_ffn_sw2"], np.float32).T),
        "sw3T": np.ascontiguousarray(np.asarray(inp["L1_ffn_sw3"], np.float32).T),
        "Wk1T": np.ascontiguousarray(np.asarray(inp["L1_attn_Wk1"], np.float32).T),
        "Wq1T": np.ascontiguousarray(np.asarray(inp["L1_attn_Wq1"], np.float32).T),
        "Wk2T": np.ascontiguousarray(np.asarray(inp["L1_attn_Wk2"], np.float32).T),
        "bk1pp": np.ascontiguousarray(np.asarray(inp["L1_attn_bk1"], np.float32).reshape(DC, 128).T),
        "bq1pp": np.ascontiguousarray(np.asarray(inp["L1_attn_bq1"], np.float32).reshape(DC, 128).T),
        "onesd": np.full((128, 1), 1.0 / D, BF),
        "onesv": np.full((128, 1), 1.0 / DV, BF),
        "ones1b": np.ones((1, 128), BF),
        "IDB": np.eye(128, dtype=BF),
    }
    in_maps = []
    for c in range(NC):
        W = np.zeros((NH, 128, NC), np.float64)
        for h in range(NH):
            for cp in range(c):
                W[h, :, cp] = 2.0 ** (S_H[h] * T * (c - 1 - cp))
        m = dict(common)
        m["WST"] = W.astype(np.float32)
        m["x1"] = r1res[c]["x1"]
        m["o_loc"] = r1res[c]["o_loc"]
        m["q_til"] = r1res[c]["q_til"]
        m["g_sil"] = r1res[c]["g_sil"]
        in_maps.append(m)
    return in_maps


def _maskc(c):
    m = np.zeros((128, NC), np.float32)
    m[:, :c] = 1.0
    return m


def make_l3_inputs(inp, r2res):
    payall = np.stack([r2res[c]["payl"] for c in range(NC)]).astype(np.float32)
    common = {
        "payall": payall,
        "WprojT": np.ascontiguousarray(np.asarray(inp["L1_attn_Wproj"], np.float32).T),
        "bprojpp": np.ascontiguousarray(np.asarray(inp["L1_attn_bproj"], np.float32).reshape(DC, 128).T),
        "bk2pp": np.ascontiguousarray(np.asarray(inp["L1_attn_bk2"], np.float32).reshape(DC, 128).T),
    }
    in_maps = []
    for c in range(NC):
        m = dict(common)
        m["x4"] = r2res[c]["x4"]; m["Qf"] = r2res[c]["Qf"]; m["Zb"] = r2res[c]["Zb"]
        m["Kcum"] = r2res[c]["Kcum"]; m["Scum"] = r2res[c]["Scum"]; m["ln5"] = r2res[c]["ln5"]
        m["prevp"] = r2res[c - 1]["payl"] if c > 0 else np.zeros((128, 4, 8), np.float32)
        m["maskc"] = _maskc(c)
        in_maps.append(m)
    return in_maps


def make_l4_inputs(inp, r3res):
    t3all = np.stack([r3res[c]["t3"] for c in range(NC)]).astype(np.float32)
    common = {
        "t3all": t3all,
        "WprojT": np.ascontiguousarray(np.asarray(inp["L1_attn_Wproj"], np.float32).T),
        "gateT": np.ascontiguousarray(np.asarray(inp["L1_ffn_gate_w"], np.float32).T),
        "ebias": np.ascontiguousarray(np.broadcast_to(np.asarray(inp["L1_ffn_expert_bias"], np.float32), (128, 8))),
        "w1T": np.ascontiguousarray(np.asarray(inp["L1_ffn_w1"], np.float32).transpose(0, 2, 1)),
        "w2T": np.ascontiguousarray(np.asarray(inp["L1_ffn_w2"], np.float32).transpose(0, 2, 1)),
        "w3T": np.ascontiguousarray(np.asarray(inp["L1_ffn_w3"], np.float32).transpose(0, 2, 1)),
        "sw1T": np.ascontiguousarray(np.asarray(inp["L1_ffn_sw1"], np.float32).T),
        "sw2T": np.ascontiguousarray(np.asarray(inp["L1_ffn_sw2"], np.float32).T),
        "sw3T": np.ascontiguousarray(np.asarray(inp["L1_ffn_sw3"], np.float32).T),
        "onesd": np.full((128, 1), 1.0 / D, BF),
        "IDB": np.eye(128, dtype=BF),
    }
    in_maps = []
    for c in range(NC):
        m = dict(common)
        m["x5p"] = r3res[c]["x5p"]; m["Q2sh"] = r3res[c]["Q2sh"]
        m["maskc"] = _maskc(c)
        in_maps.append(m)
    return in_maps


def kernel(**inp):
    nc1 = build_L1()
    r1 = run_bass_kernel_spmd(nc1, make_l1_inputs(inp), core_ids=list(range(NC)))
    r1res = [r1.results[c] for c in range(NC)]
    nc2 = build_L2()
    r2 = run_bass_kernel_spmd(nc2, make_l2_inputs(inp, r1res), core_ids=list(range(NC)))
    r2res = [r2.results[c] for c in range(NC)]
    nc3 = build_L3()
    r3 = run_bass_kernel_spmd(nc3, make_l3_inputs(inp, r2res), core_ids=list(range(NC)))
    r3res = [r3.results[c] for c in range(NC)]
    nc4 = build_L4()
    r4 = run_bass_kernel_spmd(nc4, make_l4_inputs(inp, r3res), core_ids=list(range(NC)))
    out = np.empty((SEQ, D), np.float32)
    for c in range(NC):
        out[c * T:(c + 1) * T, :] = r4.results[c]["y"]
    return out.reshape(1, SEQ, D)


# version 02
# speedup vs baseline: 1.2215x; 1.2215x over previous
import sys
if "/opt/trn_rl_repo" not in sys.path:
    sys.path.insert(0, "/opt/trn_rl_repo")
import numpy as np
import ml_dtypes
from contextlib import ExitStack
import concourse.bass as bass
import concourse.mybir as mybir
from concourse.bass_utils import run_bass_kernel_spmd

F32 = mybir.dt.float32
BF16 = mybir.dt.bfloat16
AF = mybir.ActivationFunctionType
ALU = mybir.AluOpType
AX = mybir.AxisListType
BF = ml_dtypes.bfloat16

NC = 8
SEQ = 16384
D = 1024
DC = 8
T = SEQ // NC
TS = 512
NT = T // TS
NH = 8
DQK = 128
DV = 256
VDIM = 2048
FF = 4096
FC = FF // 128
EPS_LN = 1e-5
SC_QK = DQK ** -0.5
SWAP32 = [i ^ 1 for i in range(32)]

S_H = np.log2(1.0 - np.exp2(-5.0 - np.arange(NH, dtype=np.float64)))


class Sem:
    __slots__ = ("h", "c", "name")
    def __init__(self, h, name):
        self.h = h; self.c = 0; self.name = name
    def bump(self, n):
        self.c += n
        return self.c


class Prog:
    ENGS = ("sync", "gpsimd", "vector", "scalar", "tensor")
    def __init__(self, nc):
        self.nc = nc
        self.L = {e: [] for e in self.ENGS}

    def add(self, eng, fn):
        self.L[eng].append(fn)

    def wait(self, eng, sem, val):
        if val is None or val <= 0:
            return
        self.add(eng, lambda e, s=sem.h, v=val: e.wait_ge(s, v))

    def op(self, eng, fn, sem=None, n=1):
        if sem is None:
            self.add(eng, fn)
            return None
        v = sem.bump(n)
        self.add(eng, lambda e, f=fn, s=sem.h, k=n: f(e).then_inc(s, k))
        return v

    def dma(self, queue, sem, out, in_, n=16):
        v = sem.bump(n)
        self.add(queue, lambda e, o=out, i=in_, s=sem.h, k=n: e.dma_start(out=o, in_=i).then_inc(s, k))
        return v

    def emit(self, block):
        def run(e, lst):
            for f in lst:
                f(e)
        block.sync(lambda e: run(e, self.L["sync"]))
        block.gpsimd(lambda e: run(e, self.L["gpsimd"]))
        block.vector(lambda e: run(e, self.L["vector"]))
        block.scalar(lambda e: run(e, self.L["scalar"]))
        block.tensor(lambda e: run(e, self.L["tensor"]))


class Ctx:
    def __init__(self, nc):
        self.nc = nc
        self.p = Prog(nc)
        self.stack = ExitStack()
        self._semstack = ExitStack()
        self.sems = {}
        for name in ("qL", "gL", "T", "V", "SC", "G", "ST"):
            h = self._semstack.enter_context(nc.semaphore(name))
            self.sems[name] = Sem(h, name)
        self.qL = self.sems["qL"]; self.gL = self.sems["gL"]
        self.T_ = self.sems["T"]; self.V = self.sems["V"]
        self.SC = self.sems["SC"]; self.G = self.sems["G"]; self.ST = self.sems["ST"]
        self._n = 0

    def sb(self, shape, dt=F32, stack=None):
        self._n += 1
        return (stack or self.stack).enter_context(
            self.nc.sbuf_tensor(f"sb{self._n}", shape, dt))

    def ps(self, shape, dt=F32, stack=None):
        self._n += 1
        return (stack or self.stack).enter_context(
            self.nc.psum_tensor(f"ps{self._n}", shape, dt))

    def barrier(self):
        p = self.p
        vals = [(self.T_, self.T_.c), (self.V, self.V.c), (self.SC, self.SC.c),
                (self.G, self.G.c), (self.qL, self.qL.c), (self.gL, self.gL.c),
                (self.ST, self.ST.c)]
        for eng in Prog.ENGS:
            for s, v in vals:
                p.wait(eng, s, v)


def emit_ternarize(cx, W_dram, out_dram, rows, cols, scale_half, b, consts):
    p = cx.p; qL, V, T_, SC, ST = cx.qL, cx.V, cx.T_, cx.SC, cx.ST
    nrc = rows // 128
    scrA, scrB = b["scrA"], b["scrB"]
    for i in range(nrc):
        p.wait("sync", V, V.c)
        lv = p.dma("sync", qL, b["wch"][0:128, 0:cols], W_dram[i * 128:(i + 1) * 128, :])
        p.wait("vector", qL, lv)
        p.op("vector", lambda e, i=i, c=cols: e.reduce_sum(
            b["absacc"][:, i:i + 1], b["wch"][0:128, 0:c], axis=AX.X,
            apply_absolute_value=True), sem=V)
    p.wait("sync", V, V.c)
    wv = p.dma("sync", qL, scrA, b["absacc"][:, 0:nrc])
    p.wait("sync", qL, wv)
    lv = p.dma("sync", qL, b["flat"][0:1, 0:128 * nrc], scrA.rearrange("p n -> (p n)"))
    p.wait("vector", qL, lv)
    p.op("vector", lambda e, n=nrc: e.reduce_sum(b["mval"][0:1, 0:1], b["flat"][0:1, 0:128 * n], axis=AX.X), sem=V)
    p.wait("vector", V, V.c)
    p.op("vector", lambda e: e.tensor_scalar(b["mval"][0:1, 0:1], b["mval"][0:1, 0:1],
                                             1.0 / (rows * cols), None, ALU.mult), sem=V)
    p.wait("vector", V, V.c)
    p.op("vector", lambda e: e.tensor_scalar(b["mval"][0:1, 0:1], b["mval"][0:1, 0:1], 1e-8, None, ALU.max), sem=V)
    p.wait("vector", V, V.c)
    p.op("vector", lambda e: e.tensor_scalar(b["thr2"][0:1, 0:1], b["mval"][0:1, 0:1], 0.5, None, ALU.mult))
    p.op("vector", lambda e: e.tensor_scalar(b["thr2"][0:1, 1:2], b["mval"][0:1, 0:1], -0.5, None, ALU.mult), sem=V)
    p.wait("vector", V, V.c)
    p.wait("sync", V, V.c)
    wv = p.dma("sync", qL, scrB, b["thr2"][0:1, 0:2])
    p.wait("sync", qL, wv)
    p.dma("sync", qL, b["thrp"][:, 0:1], scrB[0:1, 0:1].to_broadcast([128, 1]))
    vthr = p.dma("sync", qL, b["thrn"][:, 0:1], scrB[0:1, 1:2].to_broadcast([128, 1]))
    for i in range(nrc):
        p.wait("sync", V, V.c)
        lv = p.dma("sync", qL, b["wch"][0:128, 0:cols], W_dram[i * 128:(i + 1) * 128, :])
        p.wait("vector", qL, lv)
        p.wait("vector", qL, vthr)
        p.wait("vector", ST, ST.c)
        p.op("vector", lambda e, c=cols: e.tensor_scalar(
            b["junk"][0:128, 0:c], b["wch"][0:128, 0:c], b["thrp"][:, 0:1], None, ALU.is_ge))
        p.op("vector", lambda e, c=cols: e.tensor_scalar(
            b["wch"][0:128, 0:c], b["wch"][0:128, 0:c], b["thrn"][:, 0:1], None, ALU.is_le))
        if scale_half:
            p.op("vector", lambda e, c=cols: e.tensor_sub(
                b["junk"][0:128, 0:c], b["junk"][0:128, 0:c], b["wch"][0:128, 0:c]))
            p.op("vector", lambda e, c=cols: e.tensor_scalar(
                b["stage"][0:128, 0:c], b["junk"][0:128, 0:c], 0.5, None, ALU.mult), sem=V)
        else:
            p.op("vector", lambda e, c=cols: e.tensor_sub(
                b["stage"][0:128, 0:c], b["junk"][0:128, 0:c], b["wch"][0:128, 0:c]), sem=V)
        p.wait("gpsimd", V, V.c)
        p.dma("gpsimd", ST, out_dram[:, i, 0:cols], b["stage"][0:128, 0:cols])


def emit_ln_stats(cx, x, b, consts, pR1, pR2):
    p = cx.p; qL, V, T_, SC = cx.qL, cx.V, cx.T_, cx.SC
    for j in range(NT):
        js = j * TS
        p.wait("vector", T_, T_.c)
        for dc in range(DC):
            p.op("vector", lambda e, dc=dc, js=js: e.tensor_copy(
                b["xb2"][:, dc, :], x[:, dc, js:js + TS]), sem=(V if dc == DC - 1 else None))
        vc = cx.V.c
        p.wait("scalar", V, vc)
        for dc in range(DC):
            p.op("scalar", lambda e, dc=dc: e.activation(
                b["xsq"][:, dc, :], b["xb2"][:, dc, :], AF.Square),
                sem=(SC if dc == DC - 1 else None))
        sc = cx.SC.c
        p.wait("tensor", V, vc)
        for dc in range(DC):
            p.op("tensor", lambda e, dc=dc: e.matmul(
                pR1[0:1, 0:TS], consts["onesd_b"][:, 0:1], b["xb2"][:, dc, :],
                start=(dc == 0), stop=(dc == DC - 1)))
        p.wait("tensor", SC, sc)
        for dc in range(DC):
            p.op("tensor", lambda e, dc=dc: e.matmul(
                pR2[0:1, 0:TS], consts["onesd_b"][:, 0:1], b["xsq"][:, dc, :],
                start=(dc == 0), stop=(dc == DC - 1)), sem=(T_ if dc == DC - 1 else None))
        p.wait("scalar", T_, T_.c)
        p.op("scalar", lambda e, js=js: e.copy(b["mrow"][0:1, js:js + TS], pR1[0:1, 0:TS]))
        p.op("scalar", lambda e, js=js: e.copy(b["qrow"][0:1, js:js + TS], pR2[0:1, 0:TS]), sem=SC)
        p.wait("tensor", SC, SC.c)
    p.wait("scalar", SC, SC.c)
    p.op("scalar", lambda e: e.activation(b["srow"][0:1, :], b["mrow"][0:1, :], AF.Square), sem=SC)
    p.wait("vector", SC, SC.c)
    p.op("vector", lambda e: e.scalar_tensor_tensor(b["qrow"][0:1, :], b["qrow"][0:1, :], EPS_LN,
                                                    b["srow"][0:1, :], ALU.add, ALU.subtract), sem=V)
    p.wait("scalar", V, V.c)
    p.op("scalar", lambda e: e.activation(b["qrow"][0:1, :], b["qrow"][0:1, :], AF.Ln), sem=SC)
    p.wait("scalar", SC, SC.c)
    p.op("scalar", lambda e: e.activation(b["srow"][0:1, :], b["qrow"][0:1, :], AF.Exp, scale=-0.5), sem=SC)
    p.wait("vector", SC, SC.c)
    p.op("vector", lambda e: e.tensor_mul(b["mrow"][0:1, :], b["mrow"][0:1, :], b["srow"][0:1, :]), sem=V)
    p.wait("sync", V, V.c)
    p.wait("sync", SC, SC.c)
    wv = p.dma("sync", qL, b["lnscr"][0:1, :], b["srow"][0:1, :])
    wv = p.dma("sync", qL, b["lnscr"][1:2, :], b["mrow"][0:1, :])
    p.wait("sync", qL, wv)
    p.dma("sync", qL, b["rstd_b"][:], b["lnscr"][0:1, :].to_broadcast([128, T]))
    bq = p.dma("sync", qL, b["mr_b"][:], b["lnscr"][1:2, :].to_broadcast([128, T]))
    b["bcast_ql"] = bq
    return bq


def emit_ln_norm_j(cx, x, xln_dst, j, b, dst_js=None):
    p = cx.p; V, qL = cx.V, cx.qL
    js = j * TS
    dj = js if dst_js is None else dst_js
    p.wait("vector", qL, b["bcast_ql"])
    for dc in range(DC):
        p.op("vector", lambda e, dc=dc, js=js, dj=dj: e.tensor_mul(
            xln_dst[:, dc, dj:dj + TS], x[:, dc, js:js + TS], b["rstd_b"][:, js:js + TS]))
        p.op("vector", lambda e, dc=dc, js=js, dj=dj: e.tensor_sub(
            xln_dst[:, dc, dj:dj + TS], xln_dst[:, dc, dj:dj + TS], b["mr_b"][:, js:js + TS]),
            sem=(V if dc == DC - 1 else None))
    return cx.V.c


def emit_ffn0(cx, x, b, consts, t1_dram, t2_dram, pA, pB, st_ready, dbg=None):
    p = cx.p; qL, V, T_, SC, ST = cx.qL, cx.V, cx.T_, cx.SC, cx.ST
    last_w1 = {0: None, 1: None}
    last_w2 = {0: None, 1: None}
    banks = (pA, pB)
    guard = {0: None, 1: None}
    emit_ln_stats(cx, x, b, consts, cx.pR1, cx.pR2)
    p.wait("vector", SC, b["sc_hb2pp"])
    for j in range(NT):
        js = j * TS
        vnorm = emit_ln_norm_j(cx, x, b["xlnj"], j, b, dst_js=0)
        p.wait("tensor", V, vnorm)
        for s4 in range(8):
            bi = s4 % 2
            if last_w1[bi] is None:
                p.wait("sync", ST, st_ready)
            else:
                p.wait("sync", T_, last_w1[bi])
            lv = p.dma("sync", qL, b["w1b"][:, bi, :, :], t1_dram[:, :, s4 * 512:(s4 + 1) * 512])
            p.wait("tensor", qL, lv)
            for fo in range(4):
                fc = s4 * 4 + fo
                k = fc % 2
                pp = banks[k]
                if guard[k] is not None:
                    p.wait("tensor", guard[k][0], guard[k][1])
                for dc in range(DC):
                    p.op("tensor", lambda e, dc=dc, bi=bi, fo=fo, pp=pp: e.matmul(
                        pp[:, 0:TS], b["w1b"][:, bi, dc, fo * 128:(fo + 1) * 128],
                        b["xlnj"][:, dc, 0:TS],
                        start=(dc == 0), stop=(dc == DC - 1)), sem=(T_ if dc == DC - 1 else None))
                p.wait("scalar", T_, T_.c)
                p.op("scalar", lambda e, fc=fc, pp=pp: e.activation(
                    b["hb"][:, fc, :], pp[:, 0:TS], AF.Gelu,
                    bias=consts["b1pp"][:, fc:fc + 1]), sem=SC)
                guard[k] = (SC, cx.SC.c)
            last_w1[bi] = cx.T_.c
        p.wait("tensor", SC, cx.SC.c)
        for s4 in range(8):
            bi = s4 % 2
            dc = s4
            k = s4 % 2
            pp = banks[k]
            if last_w2[bi] is None:
                p.wait("sync", ST, st_ready)
            else:
                p.wait("sync", T_, last_w2[bi])
            lv = p.dma("sync", qL, b["w2b"][:, bi, :, :], t2_dram[:, :, s4 * 128:(s4 + 1) * 128])
            p.wait("tensor", qL, lv)
            if guard[k] is not None:
                p.wait("tensor", guard[k][0], guard[k][1])
            for fc in range(FC):
                p.op("tensor", lambda e, fc=fc, bi=bi, pp=pp: e.matmul(
                    pp[:, 0:TS], b["w2b"][:, bi, fc, :],
                    b["hb"][:, fc, :],
                    start=(fc == 0), stop=(fc == FC - 1)), sem=(T_ if fc == FC - 1 else None))
            last_w2[bi] = cx.T_.c
            p.wait("vector", T_, T_.c)
            p.op("vector", lambda e, dc=dc, js=js, pp=pp: e.scalar_tensor_tensor(
                x[:, dc, js:js + TS], pp[:, 0:TS], b["hb2pp"][:, dc:dc + 1],
                x[:, dc, js:js + TS], ALU.add, ALU.add), sem=V)
            guard[k] = (V, cx.V.c)
        p.wait("tensor", V, cx.V.c)


def ln_bufs(cx, stack):
    return dict(
        xb2=cx.sb([128, DC, TS], BF16, stack), xsq=cx.sb([128, DC, TS], BF16, stack),
        mrow=cx.sb([1, T], F32, stack), qrow=cx.sb([1, T], F32, stack),
        srow=cx.sb([1, T], F32, stack),
    )


def build_L1():
    nc = bass.Bass("TRN2", target_bir_lowering=False, debug=False)
    ap = lambda *a, **k: nc.dram_tensor(*a, **k).ap()
    x0 = ap("x0", [D, T], F32, kind="ExternalInput")
    W1T = ap("W1T", [D, FF], F32, kind="ExternalInput")
    W2T = ap("W2T", [FF, D], F32, kind="ExternalInput")
    b1_ = ap("b1pp", [128, FC], F32, kind="ExternalInput")
    b2_ = ap("b2pp", [128, DC], F32, kind="ExternalInput")
    Wq = ap("Wq", [D, D], F32, kind="ExternalInput")
    Wk = ap("Wk", [D, D], F32, kind="ExternalInput")
    Wv = ap("Wv", [D, VDIM], F32, kind="ExternalInput")
    Wg = ap("Wg", [D, VDIM], F32, kind="ExternalInput")
    COS = ap("COS", [128, T], F32, kind="ExternalInput")
    SIN = ap("SIN", [128, T], F32, kind="ExternalInput")
    DT_ = ap("DT", [NH, 4, 128, TS], BF16, kind="ExternalInput")
    WIN = ap("WIN", [NH, 128, TS], BF16, kind="ExternalInput")
    KHAT = ap("KHAT", [NH, 128, 4], F32, kind="ExternalInput")
    IDB = ap("IDB", [128, 128], BF16, kind="ExternalInput")
    onesd_ = ap("onesd", [128, 1], BF16, kind="ExternalInput")
    ones1f_ = ap("ones1f", [1, 128], F32, kind="ExternalInput")
    onesNf_ = ap("onesNf", [128, 1], F32, kind="ExternalInput")

    x1_o = ap("x1", [D, T], F32, kind="ExternalOutput")
    t1_o = ap("T1b", [128, DC, FF], BF16, kind="ExternalOutput")
    t2_o = ap("T2b", [128, FC, D], BF16, kind="ExternalOutput")
    oloc_o = ap("o_loc", [VDIM, T], F32, kind="ExternalOutput")
    qtil_o = ap("q_til", [D, T], BF16, kind="ExternalOutput")
    gsil_o = ap("g_sil", [VDIM, T], BF16, kind="ExternalOutput")
    sloc_o = ap("S_loc", [NH, 128, DV], F32, kind="ExternalOutput")

    scrA1 = ap("scrA1", [128, 8], F32, kind="Internal")
    scrA2 = ap("scrA2", [128, 32], F32, kind="Internal")
    scrB1 = ap("scrB1", [1, 2], F32, kind="Internal")
    scrB2 = ap("scrB2", [1, 2], F32, kind="Internal")
    lnscr = ap("lnscr", [2, T], F32, kind="Internal")

    cx = Ctx(nc)
    p = cx.p
    qL, gL, T_, V, SC, G, ST = cx.qL, cx.gL, cx.T_, cx.V, cx.SC, cx.G, cx.ST

    x = cx.sb([128, DC, T])
    consts = dict(
        onesd_b=cx.sb([128, 1], BF16), ones1f=cx.sb([1, 128]),
        onesNf=cx.sb([128, 1]), b1pp=cx.sb([128, FC]), b2pp=cx.sb([128, DC]),
    )
    cx.pR1 = cx.ps([1, TS]); cx.pR2 = cx.ps([1, TS])
    pA = cx.ps([128, TS]); pB = cx.ps([128, TS])
    pC = cx.ps([128, TS]); pD = cx.ps([128, TS])

    with (nc.Block() as block,):
        lx = p.dma("sync", qL, x[:], x0.rearrange("(dc q) t -> q dc t", q=128))
        p.dma("sync", qL, consts["ones1f"][:], ones1f_)
        p.dma("sync", qL, consts["onesNf"][:], onesNf_)
        p.dma("sync", qL, consts["b1pp"][:], b1_)
        p.dma("sync", qL, consts["b2pp"][:], b2_)
        lcb = p.dma("sync", qL, consts["onesd_b"][:], onesd_)

        with ExitStack() as ph:
            tb = dict(wch=cx.sb([128, FF], F32, ph), junk=cx.sb([128, FF], F32, ph),
                      stage=cx.sb([128, FF], BF16, ph), absacc=cx.sb([128, 64], F32, ph),
                      mval=cx.sb([1, 1], F32, ph), thrp=cx.sb([128, 1], F32, ph),
                      thrn=cx.sb([128, 1], F32, ph), thr2=cx.sb([1, 2], F32, ph),
                      flat=cx.sb([1, 128 * 32], F32, ph))
            p.wait("vector", qL, lcb)
            tb["scrA"] = scrA1; tb["scrB"] = scrB1
            emit_ternarize(cx, W1T, t1_o, D, FF, False, tb, consts)
            tb["scrA"] = scrA2; tb["scrB"] = scrB2
            emit_ternarize(cx, W2T, t2_o, FF, D, True, tb, consts)
            cx.barrier()

        with ExitStack() as ph:
            b = ln_bufs(cx, ph)
            b["rstd_b"] = cx.sb([128, T], F32, ph)
            b["mr_b"] = cx.sb([128, T], F32, ph)
            b["lnscr"] = lnscr
            b["xlnj"] = cx.sb([128, DC, TS], BF16, ph)
            b["hb"] = cx.sb([128, FC, TS], BF16, ph)
            b["w1b"] = cx.sb([128, 2, DC, 512], BF16, ph)
            b["w2b"] = cx.sb([128, 2, FC, 128], BF16, ph)
            b["hb2pp"] = cx.sb([128, DC], F32, ph)
            p.wait("scalar", qL, lcb)
            b["sc_hb2pp"] = p.op("scalar", lambda e: e.mul(b["hb2pp"][:, 0:DC], consts["b2pp"][:, 0:DC], 0.5), sem=SC)
            p.wait("vector", qL, lx)
            st_tern = cx.ST.c
            emit_ffn0(cx, x, b, consts, t1_o, t2_o, pA, pB, st_tern)
            p.wait("sync", V, V.c)
            p.dma("sync", ST, x1_o.rearrange("(dc q) t -> q dc t", q=128), x[:])
            cx.barrier()

        with ExitStack() as ph:
            xlnb = cx.sb([128, DC, T], BF16, ph)
            b3 = ln_bufs(cx, ph)
            b3["rstd_b"] = cx.sb([128, T], F32, ph)
            b3["mr_b"] = cx.sb([128, T], F32, ph)
            b3["lnscr"] = lnscr
            emit_ln_stats(cx, x, b3, consts, cx.pR1, cx.pR2)
            for j in range(NT):
                emit_ln_norm_j(cx, x, xlnb, j, b3)
            cx.barrier()

            cosb = cx.sb([128, T], F32, ph)
            sinb = cx.sb([128, T], F32, ph)
            idb = cx.sb([128, 128], BF16, ph)
            wqh = cx.sb([128, DC, 128], BF16, ph)
            wkh = cx.sb([128, DC, 128], BF16, ph)
            wvh = cx.sb([128, DC, DV], BF16, ph)
            wgh = cx.sb([128, DC, DV], BF16, ph)
            dth = cx.sb([128, 4, TS], BF16, ph)
            winh = cx.sb([128, TS], BF16, ph)
            khath = cx.sb([128, 4], F32, ph)
            qf = cx.sb([128, TS], F32, ph)
            qsw = cx.sb([128, TS], F32, ph)
            qrot = cx.sb([128, TS], BF16, ph)
            krot = cx.sb([128, TS], BF16, ph)
            qtb = cx.sb([128, TS], BF16, ph)
            vtb = cx.sb([128, 4, DV], BF16, ph)
            ktm = cx.sb([128, 4, 128], BF16, ph)
            ssb = cx.sb([128, 4, TS], BF16, ph)
            Sst = cx.sb([128, DV], F32, ph)
            Ssb = cx.sb([128, DV], BF16, ph)
            ostg = cx.sb([128, 2, TS], F32, ph)
            gstg = cx.sb([128, 2, TS], BF16, ph)
            pT = cx.ps([128, 128], BF16, ph)
            pS = cx.ps([128, DV], F32, ph)

            p.dma("gpsimd", gL, cosb[:], COS)
            p.dma("gpsimd", gL, sinb[:], SIN)
            lid = p.dma("gpsimd", gL, idb[:], IDB)
            p.wait("tensor", gL, lid)
            p.wait("vector", gL, lid)

            lam = {h: float(2.0 ** (S_H[h] * TS)) for h in range(NH)}
            for h in range(NH):
                p.wait("gpsimd", T_, T_.c)
                p.dma("gpsimd", gL, wqh[:], Wq.rearrange("(dc q) e -> q dc e", q=128)[:, :, h * 128:(h + 1) * 128])
                p.dma("gpsimd", gL, wkh[:], Wk.rearrange("(dc q) e -> q dc e", q=128)[:, :, h * 128:(h + 1) * 128])
                p.dma("gpsimd", gL, wvh[:], Wv.rearrange("(dc q) e -> q dc e", q=128)[:, :, h * DV:(h + 1) * DV])
                lwg = p.dma("gpsimd", gL, wgh[:], Wg.rearrange("(dc q) e -> q dc e", q=128)[:, :, h * DV:(h + 1) * DV])
                p.wait("sync", V, V.c)
                p.dma("sync", qL, dth[:], DT_[h].rearrange("jb p i -> p jb i"))
                p.dma("sync", qL, winh[:], WIN[h])
                lkh = p.dma("sync", qL, khath[:], KHAT[h])
                p.wait("vector", ST, ST.c)
                p.op("vector", lambda e: e.memset(Sst[:], 0.0), sem=V)
                p.wait("tensor", gL, lwg)
                p.wait("tensor", qL, lkh)
                p.wait("vector", qL, lkh)
                p.wait("tensor", V, V.c)
                st_prev = cx.ST.c
                for ci in range(NT):
                    js = ci * TS
                    p.wait("vector", ST, st_prev)
                    p.wait("scalar", ST, st_prev)
                    for (wt, rot) in ((wqh, qrot), (wkh, krot)):
                        for dc in range(DC):
                            p.op("tensor", lambda e, dc=dc, js=js, wt=wt: e.matmul(
                                pA[:, 0:TS], wt[:, dc, :], xlnb[:, dc, js:js + TS],
                                start=(dc == 0), stop=(dc == DC - 1)), sem=(T_ if dc == DC - 1 else None))
                        p.wait("vector", T_, T_.c)
                        p.op("vector", lambda e: e.tensor_copy(qf[:], pA[:, 0:TS]), sem=V)
                        p.wait("vector", V, V.c)
                        p.op("vector", lambda e: e.stream_shuffle(qsw[:], qf[:], SWAP32))
                        p.op("vector", lambda e, js=js: e.tensor_mul(qf[:], qf[:], cosb[:, js:js + TS]))
                        p.op("vector", lambda e, js=js: e.tensor_mul(qsw[:], qsw[:], sinb[:, js:js + TS]))
                        p.op("vector", lambda e, rot=rot: e.tensor_add(rot[:], qf[:], qsw[:]), sem=V)
                        p.wait("tensor", V, V.c)
                    p.op("vector", lambda e: e.tensor_mul(qtb[:], qrot[:], winh[:]), sem=V)
                    for tb_ in range(4):
                        for dc in range(DC):
                            p.op("tensor", lambda e, tb_=tb_, dc=dc, js=js: e.matmul(
                                pS[:, 0:DV], xlnb[:, dc, js + tb_ * 128:js + (tb_ + 1) * 128],
                                wvh[:, dc, :],
                                start=(dc == 0), stop=(dc == DC - 1)), sem=(T_ if dc == DC - 1 else None))
                        p.wait("vector", T_, T_.c)
                        p.op("vector", lambda e, tb_=tb_: e.tensor_copy(vtb[:, tb_, :], pS[:, 0:DV]), sem=V)
                        p.wait("tensor", V, V.c)
                    for jb in range(4):
                        p.op("tensor", lambda e, jb=jb: e.matmul(
                            pB[:, 0:TS], krot[:, jb * 128:(jb + 1) * 128], qrot[:],
                            start=True, stop=True), sem=T_)
                        p.wait("vector", T_, T_.c)
                        p.op("vector", lambda e, jb=jb: e.tensor_mul(
                            ssb[:, jb, :], pB[:, 0:TS], dth[:, jb, :]), sem=V)
                        p.wait("tensor", V, V.c)
                    for jb in range(4):
                        p.op("tensor", lambda e, jb=jb: e.transpose(
                            pT[:], krot[:, jb * 128:(jb + 1) * 128], idb[:]), sem=T_)
                        p.wait("vector", T_, T_.c)
                        p.op("vector", lambda e, jb=jb: e.tensor_scalar(
                            ktm[:, jb, :], pT[:], khath[:, jb:jb + 1], None, ALU.mult), sem=V)
                        p.wait("tensor", V, V.c)
                    p.op("vector", lambda e: e.tensor_copy(Ssb[:], Sst[:]), sem=V)
                    p.wait("tensor", V, V.c)
                    for vb in range(2):
                        for jb in range(4):
                            p.op("tensor", lambda e, vb=vb, jb=jb: e.matmul(
                                pA[:, 0:TS], vtb[:, jb, vb * 128:(vb + 1) * 128], ssb[:, jb, :],
                                start=(jb == 0), stop=False))
                        p.op("tensor", lambda e, vb=vb: e.matmul(
                            pA[:, 0:TS], Ssb[:, vb * 128:(vb + 1) * 128], qtb[:],
                            start=False, stop=True), sem=T_)
                        p.wait("vector", T_, T_.c)
                        p.op("vector", lambda e, vb=vb: e.tensor_copy(ostg[:, vb, :], pA[:, 0:TS]), sem=V)
                        p.wait("tensor", V, V.c)
                    p.wait("sync", V, V.c)
                    for vb in range(2):
                        p.dma("sync", ST, oloc_o[h * DV + vb * 128: h * DV + (vb + 1) * 128, js:js + TS],
                              ostg[:, vb, :])
                    p.dma("sync", ST, qtil_o[h * 128:(h + 1) * 128, js:js + TS], qtb[:])
                    for vb in range(2):
                        for dc in range(DC):
                            p.op("tensor", lambda e, vb=vb, dc=dc, js=js: e.matmul(
                                pB[:, 0:TS], wgh[:, dc, vb * 128:(vb + 1) * 128],
                                xlnb[:, dc, js:js + TS],
                                start=(dc == 0), stop=(dc == DC - 1)), sem=(T_ if dc == DC - 1 else None))
                        p.wait("scalar", T_, T_.c)
                        p.op("scalar", lambda e, vb=vb: e.activation(
                            gstg[:, vb, :], pB[:, 0:TS], AF.Silu), sem=SC)
                        p.wait("tensor", SC, SC.c)
                    p.wait("sync", SC, SC.c)
                    for vb in range(2):
                        p.dma("sync", ST, gsil_o[h * DV + vb * 128: h * DV + (vb + 1) * 128, js:js + TS],
                              gstg[:, vb, :])
                    for jb in range(4):
                        p.op("tensor", lambda e, jb=jb: e.matmul(
                            pS[:, 0:DV], ktm[:, jb, :], vtb[:, jb, :],
                            start=(jb == 0), stop=(jb == 3)), sem=(T_ if jb == 3 else None))
                    p.wait("vector", T_, T_.c)
                    p.op("vector", lambda e, h=h: e.tensor_scalar(Sst[:], Sst[:], lam[h], None, ALU.mult), sem=V)
                    p.wait("vector", V, V.c)
                    p.op("vector", lambda e: e.tensor_add(Sst[:], Sst[:], pS[:, 0:DV]), sem=V)
                    p.wait("tensor", V, V.c)
                    st_prev = cx.ST.c
                p.wait("sync", V, V.c)
                p.dma("sync", ST, sloc_o[h], Sst[:])

        p.wait("sync", ST, ST.c)
        p.emit(block)
    cx.stack.close()
    cx._semstack.close()
    return nc


def emit_moe(cx, x, xlnb, m, consts):
    p = cx.p
    qL, gL, T_, V, SC, G, ST = cx.qL, cx.gL, cx.T_, cx.V, cx.SC, cx.G, cx.ST
    pAk = (m["pA"], m["pA2"]); pDk = (m["pD"], m["pD2"]); pCk = (m["pC"], m["pC2"])
    pT = m["pT"]
    lg = p.dma("gpsimd", gL, m["gateb"][:], m["gateT_dram"].rearrange("(dc q) e -> q dc e", q=128))
    p.wait("tensor", gL, lg)
    gR = {0: None, 1: None}
    for j in range(NT):
        js = j * TS
        for tb in range(4):
            k = tb % 2
            pa = pAk[k]
            p.wait("tensor", SC, gR[k])
            for dc in range(DC):
                p.op("tensor", lambda e, tb=tb, dc=dc, js=js, pa=pa: e.matmul(
                    pa[:, 0:8], xlnb[:, dc, js + tb * 128:js + (tb + 1) * 128],
                    m["gateb"][:, dc, :],
                    start=(dc == 0), stop=(dc == DC - 1)), sem=(T_ if dc == DC - 1 else None))
            p.wait("scalar", T_, T_.c)
            p.wait("scalar", V, V.c)
            p.op("scalar", lambda e, pa=pa: e.activation(m["s_tm"][:, 0:8], pa[:, 0:8], AF.Sigmoid), sem=SC)
            gR[k] = cx.SC.c
            p.wait("vector", SC, SC.c)
            p.op("vector", lambda e: e.tensor_add(m["b_tm"][:, 0:8], m["s_tm"][:, 0:8],
                                                  consts["ebias"][:, 0:8]), sem=V)
            p.wait("vector", V, V.c)
            p.op("vector", lambda e: e.reduce_max(m["m1"][:, 0:1], m["b_tm"][:, 0:8], axis=AX.X), sem=V)
            p.wait("vector", V, V.c)
            p.op("vector", lambda e: e.tensor_scalar(m["mk1"][:, 0:8], m["b_tm"][:, 0:8],
                                                     m["m1"][:, 0:1], None, ALU.is_ge), sem=V)
            p.wait("vector", V, V.c)
            p.op("vector", lambda e: e.scalar_tensor_tensor(m["b_tm"][:, 0:8], m["mk1"][:, 0:8],
                                                            -1e30, m["b_tm"][:, 0:8],
                                                            ALU.mult, ALU.add), sem=V)
            p.wait("vector", V, V.c)
            p.op("vector", lambda e: e.reduce_max(m["m1"][:, 0:1], m["b_tm"][:, 0:8], axis=AX.X), sem=V)
            p.wait("vector", V, V.c)
            p.op("vector", lambda e: e.tensor_scalar(m["mk2"][:, 0:8], m["b_tm"][:, 0:8],
                                                     m["m1"][:, 0:1], None, ALU.is_ge), sem=V)
            p.wait("vector", V, V.c)
            p.op("vector", lambda e: e.tensor_add(m["mk1"][:, 0:8], m["mk1"][:, 0:8], m["mk2"][:, 0:8]))
            p.op("vector", lambda e: e.tensor_mul(m["sc_tmb"][:, 0:8], m["mk1"][:, 0:8], m["s_tm"][:, 0:8]), sem=V)
            p.wait("tensor", V, V.c)
            p.op("tensor", lambda e: e.transpose(pT[0:8, 0:128], m["sc_tmb"][:, 0:8], consts["idb"][:]), sem=T_)
            p.wait("vector", T_, T_.c)
            p.op("vector", lambda e, js=js, tb=tb: e.tensor_copy(
                m["sc_fm"][0:8, js + tb * 128:js + (tb + 1) * 128], pT[0:8, 0:128]), sem=V)
            p.wait("tensor", V, V.c)
    p.op("vector", lambda e: e.tensor_mul(m["s2_fm"][0:8, :], m["sc_fm"][0:8, :], m["sc_fm"][0:8, :]), sem=V)
    p.wait("vector", V, V.c)
    p.op("vector", lambda e: e.tensor_scalar(m["s2_fm"][0:8, :], m["s2_fm"][0:8, :], 0.5, None, ALU.mult), sem=V)
    p.wait("sync", V, V.c)
    p.dma("sync", qL, m["sc_scr"][0], m["sc_fm"][0:8, :])
    vroute = p.dma("sync", qL, m["sc_scr"][1], m["s2_fm"][0:8, :])
    p.wait("sync", qL, vroute)

    def load_ab(e_):
        if e_ == 8:
            p.dma("gpsimd", gL, m["wA"][:], m["sw1T_dram"].rearrange("(dc q) h -> q dc h", q=128))
            return p.dma("gpsimd", gL, m["wB"][:], m["sw3T_dram"].rearrange("(dc q) h -> q dc h", q=128))
        p.dma("gpsimd", gL, m["wA"][:], m["w1T_dram"][e_].rearrange("(dc q) h -> q dc h", q=128))
        return p.dma("gpsimd", gL, m["wB"][:], m["w3T_dram"][e_].rearrange("(dc q) h -> q dc h", q=128))

    def load_c(e_):
        if e_ == 8:
            return p.dma("gpsimd", gL, m["wC"][:], m["sw2T_dram"].rearrange("(hc q) d -> q hc d", q=128))
        return p.dma("gpsimd", gL, m["wC"][:], m["w2T_dram"][e_].rearrange("(hc q) d -> q hc d", q=128))

    def bcast(e_, j, k):
        js = j * TS
        p.wait("sync", V, m["gin"][k])
        p.dma("sync", qL, m["in1b"][:, k, :], m["sc_scr"][0, e_:e_ + 1, js:js + TS].to_broadcast([128, TS]))
        return p.dma("sync", qL, m["in2b"][:, k, :], m["sc_scr"][1, e_:e_ + 1, js:js + TS].to_broadcast([128, TS]))

    m["gin"] = {0: None, 1: None}
    gA = {0: None, 1: None}; gD = {0: None, 1: None}; gC = {0: None, 1: None}
    lab = load_ab(0)
    lc = load_c(0)
    bq = {0: bcast(0, 0, 0)}
    for e_ in range(9):
        shared = (e_ == 8)
        p.wait("tensor", gL, lab)
        scw = None
        if shared:
            p.wait("vector", gL, lc)
            for hc in range(DC):
                p.op("vector", lambda e, hc=hc: e.tensor_scalar(
                    m["wC"][:, hc, :], m["wC"][:, hc, :], 0.5, None, ALU.mult),
                    sem=(V if hc == DC - 1 else None))
            scw = cx.V.c
        for j in range(NT):
            js = j * TS
            if not shared:
                k = j % 2
                p.wait("vector", qL, bq[j])
                nxt = (e_, j + 1) if j + 1 < NT else ((e_ + 1, 0) if e_ + 1 < 8 else None)
                for dc in range(DC):
                    p.op("vector", lambda e, dc=dc, js=js, k=k: e.tensor_mul(
                        m["xin1"][:, dc, :], xlnb[:, dc, js:js + TS], m["in1b"][:, k, :]))
                    p.op("vector", lambda e, dc=dc, js=js, k=k: e.tensor_mul(
                        m["xin2"][:, dc, :], xlnb[:, dc, js:js + TS], m["in2b"][:, k, :]),
                        sem=(V if dc == DC - 1 else None))
                m["gin"][k] = cx.V.c
                if nxt is not None:
                    bq[nxt[1] if nxt[0] == e_ else 0] = bcast(nxt[0], nxt[1], (j + 1) % 2)
                    if nxt[0] != e_:
                        bq[0] = bq.get(0)
                p.wait("tensor", V, m["gin"][k])
            for hc in range(DC):
                kk = hc % 2
                pa = pAk[kk]; pd = pDk[kk]
                p.wait("tensor", SC, gA[kk])
                p.wait("tensor", V, gD[kk])
                for dc in range(DC):
                    if shared:
                        p.op("tensor", lambda e, dc=dc, hc=hc, js=js, pa=pa: e.matmul(
                            pa[:, 0:TS], m["wA"][:, dc, hc * 128:(hc + 1) * 128],
                            xlnb[:, dc, js:js + TS],
                            start=(dc == 0), stop=(dc == DC - 1)))
                    else:
                        p.op("tensor", lambda e, dc=dc, hc=hc, pa=pa: e.matmul(
                            pa[:, 0:TS], m["wA"][:, dc, hc * 128:(hc + 1) * 128],
                            m["xin1"][:, dc, :],
                            start=(dc == 0), stop=(dc == DC - 1)))
                for dc in range(DC):
                    if shared:
                        p.op("tensor", lambda e, dc=dc, hc=hc, js=js, pd=pd: e.matmul(
                            pd[:, 0:TS], m["wB"][:, dc, hc * 128:(hc + 1) * 128],
                            xlnb[:, dc, js:js + TS],
                            start=(dc == 0), stop=(dc == DC - 1)),
                            sem=(T_ if dc == DC - 1 else None))
                    else:
                        p.op("tensor", lambda e, dc=dc, hc=hc, pd=pd: e.matmul(
                            pd[:, 0:TS], m["wB"][:, dc, hc * 128:(hc + 1) * 128],
                            m["xin2"][:, dc, :],
                            start=(dc == 0), stop=(dc == DC - 1)),
                            sem=(T_ if dc == DC - 1 else None))
                thc = cx.T_.c
                if j == NT - 1 and hc == DC - 1 and e_ < 8:
                    p.wait("gpsimd", T_, thc)
                    lab = load_ab(e_ + 1)
                p.wait("scalar", T_, thc)
                p.wait("scalar", V, gD[kk])
                p.op("scalar", lambda e, kk=kk, pa=pa: e.activation(
                    m["sa"][:, kk, :], pa[:, 0:TS], AF.Silu), sem=SC)
                gA[kk] = cx.SC.c
                p.wait("vector", SC, gA[kk])
                p.op("vector", lambda e, hc=hc, kk=kk, pd=pd: e.tensor_mul(
                    m["hbm"][:, hc, :], m["sa"][:, kk, :], pd[:, 0:TS]), sem=V)
                gD[kk] = cx.V.c
            p.wait("tensor", V, cx.V.c)
            p.wait("tensor", gL, lc)
            if shared and scw is not None:
                p.wait("tensor", V, scw)
            for dc in range(DC):
                kk = dc % 2
                pc = pCk[kk]
                p.wait("tensor", V, gC[kk])
                for hc in range(DC):
                    p.op("tensor", lambda e, hc=hc, dc=dc, pc=pc: e.matmul(
                        pc[:, 0:TS], m["wC"][:, hc, dc * 128:(dc + 1) * 128],
                        m["hbm"][:, hc, :],
                        start=(hc == 0), stop=(hc == DC - 1)), sem=(T_ if hc == DC - 1 else None))
                p.wait("vector", T_, T_.c)
                p.op("vector", lambda e, dc=dc, js=js, pc=pc: e.tensor_add(
                    x[:, dc, js:js + TS], x[:, dc, js:js + TS], pc[:, 0:TS]), sem=V)
                gC[kk] = cx.V.c
        if e_ < 8:
            p.wait("gpsimd", T_, T_.c)
            lc = load_c(e_ + 1)


def moe_bufs(cx, ph):
    return dict(
        gateb=cx.sb([128, DC, 8], BF16, ph),
        s_tm=cx.sb([128, 8], F32, ph), b_tm=cx.sb([128, 8], F32, ph),
        m1=cx.sb([128, 1], F32, ph), mk1=cx.sb([128, 8], F32, ph), mk2=cx.sb([128, 8], F32, ph),
        sc_tmb=cx.sb([128, 8], BF16, ph),
        sc_fm=cx.sb([8, T], F32, ph), s2_fm=cx.sb([8, T], F32, ph),
        in1b=cx.sb([128, 2, TS], F32, ph), in2b=cx.sb([128, 2, TS], F32, ph),
        xin1=cx.sb([128, DC, TS], BF16, ph), xin2=cx.sb([128, DC, TS], BF16, ph),
        wA=cx.sb([128, DC, 1024], BF16, ph), wB=cx.sb([128, DC, 1024], BF16, ph),
        wC=cx.sb([128, DC, 1024], BF16, ph),
        sa=cx.sb([128, 2, TS], BF16, ph), hbm=cx.sb([128, DC, TS], BF16, ph),
    )


def build_L2():
    nc = bass.Bass("TRN2", target_bir_lowering=False, debug=False)
    ap = lambda *a, **k: nc.dram_tensor(*a, **k).ap()
    x1_i = ap("x1", [D, T], F32, kind="ExternalInput")
    oloc_i = ap("o_loc", [VDIM, T], F32, kind="ExternalInput")
    qtil_i = ap("q_til", [D, T], BF16, kind="ExternalInput")
    gsil_i = ap("g_sil", [VDIM, T], BF16, kind="ExternalInput")
    sall_i = ap("S_all", [NC, NH, 128, DV], F32, kind="ExternalInput")
    wst_i = ap("WST", [NH, 128, NC], F32, kind="ExternalInput")
    Wo_i = ap("Wo", [VDIM, D], F32, kind="ExternalInput")
    gnw_i = ap("gnw2", [128, 2], F32, kind="ExternalInput")
    t1_i = ap("T1b", [128, DC, FF], BF16, kind="ExternalInput")
    t2_i = ap("T2b", [128, FC, D], BF16, kind="ExternalInput")
    b1_ = ap("b1pp", [128, FC], F32, kind="ExternalInput")
    b2_ = ap("b2pp", [128, DC], F32, kind="ExternalInput")
    gateT_i = ap("gateT", [D, 8], F32, kind="ExternalInput")
    ebias_i = ap("ebias", [128, 8], F32, kind="ExternalInput")
    w1T_i = ap("w1T", [8, D, D], F32, kind="ExternalInput")
    w2T_i = ap("w2T", [8, D, D], F32, kind="ExternalInput")
    w3T_i = ap("w3T", [8, D, D], F32, kind="ExternalInput")
    sw1T_i = ap("sw1T", [D, D], F32, kind="ExternalInput")
    sw2T_i = ap("sw2T", [D, D], F32, kind="ExternalInput")
    sw3T_i = ap("sw3T", [D, D], F32, kind="ExternalInput")
    Wk1T_i = ap("Wk1T", [D, D], F32, kind="ExternalInput")
    Wq1T_i = ap("Wq1T", [D, D], F32, kind="ExternalInput")
    Wk2T_i = ap("Wk2T", [D, D], F32, kind="ExternalInput")
    bk1_ = ap("bk1pp", [128, DC], F32, kind="ExternalInput")
    bq1_ = ap("bq1pp", [128, DC], F32, kind="ExternalInput")
    onesd_ = ap("onesd", [128, 1], BF16, kind="ExternalInput")
    onesv_ = ap("onesv", [128, 1], BF16, kind="ExternalInput")
    ones1b_ = ap("ones1b", [1, 128], BF16, kind="ExternalInput")
    idb_ = ap("IDB", [128, 128], BF16, kind="ExternalInput")

    x4_o = ap("x4", [D, T], F32, kind="ExternalOutput")
    qf_o = ap("Qf", [D, T], F32, kind="ExternalOutput")
    zb_o = ap("Zb", [D, T], BF16, kind="ExternalOutput")
    kcum_o = ap("Kcum", [D, T], F32, kind="ExternalOutput")
    scum_o = ap("Scum", [D, T], F32, kind="ExternalOutput")
    ln5_o = ap("ln5", [2, T], F32, kind="ExternalOutput")
    payl_o = ap("payl", [128, 4, 8], F32, kind="ExternalOutput")

    lnscr = ap("lnscr", [2, T], F32, kind="Internal")
    scscr = ap("scscr", [2, 8, T], F32, kind="Internal")

    cx = Ctx(nc)
    p = cx.p
    qL, gL, T_, V, SC, G, ST = cx.qL, cx.gL, cx.T_, cx.V, cx.SC, cx.G, cx.ST

    x = cx.sb([128, DC, T])
    consts = dict(
        onesd_b=cx.sb([128, 1], BF16), onesv_b=cx.sb([128, 1], BF16),
        ones1b=cx.sb([1, 128], BF16), idb=cx.sb([128, 128], BF16),
        b1pp=cx.sb([128, FC]), b2pp=cx.sb([128, DC]),
        ebias=cx.sb([128, 8]), bk1pp=cx.sb([128, DC]), bq1pp=cx.sb([128, DC]),
    )
    pA = cx.ps([128, TS]); pB = cx.ps([128, TS])
    pC = cx.ps([128, TS]); pD = cx.ps([128, TS])

    with (nc.Block() as block,):
        lx = p.dma("sync", qL, x[:], x1_i.rearrange("(dc q) t -> q dc t", q=128))
        p.dma("sync", qL, consts["onesd_b"][:], onesd_)
        p.dma("sync", qL, consts["onesv_b"][:], onesv_)
        p.dma("sync", qL, consts["ones1b"][:], ones1b_)
        p.dma("sync", qL, consts["b1pp"][:], b1_)
        p.dma("sync", qL, consts["b2pp"][:], b2_)
        p.dma("sync", qL, consts["ebias"][:], ebias_i)
        p.dma("sync", qL, consts["bk1pp"][:], bk1_)
        p.dma("sync", qL, consts["bq1pp"][:], bq1_)
        lcb = p.dma("sync", qL, consts["idb"][:], idb_)
        p.wait("vector", qL, lcb)

        with ExitStack() as ph:
            prow = cx.ps([1, TS], stack=ph)
            sallb = cx.sb([128, NC, DV], F32, ph)
            Sin = cx.sb([128, NH, DV], F32, ph)
            wstc = cx.sb([128, NC], F32, ph)
            SinSb = cx.sb([128, DV], BF16, ph)
            qt_t = cx.sb([128, TS], BF16, ph)
            olc = cx.sb([128, 2, TS], F32, ph)
            gsl = cx.sb([128, 2, TS], BF16, ph)
            ostg = cx.sb([128, 2, TS], F32, ph)
            o2b = cx.sb([128, TS], BF16, ph)
            grow = cx.sb([1, TS], F32, ph)
            growb = cx.sb([1, TS], BF16, ph)
            ogall = cx.sb([128, 16, TS], BF16, ph)
            wob = cx.sb([128, 16, D], BF16, ph)
            gnwc = cx.sb([128, 2], F32, ph)

            for h in range(NH):
                lv = p.dma("sync", qL, sallb[:], sall_i[:, h].rearrange("c p v -> p c v"))
                lw = p.dma("sync", qL, wstc[:], wst_i[h])
                p.wait("vector", qL, lw)
                p.op("vector", lambda e, h=h: e.memset(Sin[:, h, :], 0.0), sem=V)
                p.wait("vector", V, V.c)
                for c_ in range(NC):
                    p.op("vector", lambda e, h=h, c_=c_: e.scalar_tensor_tensor(
                        Sin[:, h, :], sallb[:, c_, :], wstc[:, c_:c_ + 1], Sin[:, h, :],
                        ALU.mult, ALU.add), sem=V)
                    p.wait("vector", V, V.c)
                p.wait("sync", V, V.c)
            lwo = p.dma("gpsimd", gL, wob[:], Wo_i.rearrange("(vc q) d -> q vc d", q=128))
            lgn = p.dma("sync", qL, gnwc[:], gnw_i)
            p.wait("vector", gL, lwo)
            p.wait("vector", qL, lgn)
            for vc in range(16):
                p.op("vector", lambda e, vc=vc: e.tensor_scalar(
                    wob[:, vc, :], wob[:, vc, :], gnwc[:, (vc % 2):(vc % 2) + 1], None, ALU.mult),
                    sem=(V if vc == 15 else None))
            vwo = cx.V.c
            p.wait("tensor", V, vwo)

            lam512 = {(h, ci): float(2.0 ** (S_H[h] * TS * ci)) for h in range(NH) for ci in range(NT)}
            for ci in range(NT):
                js = ci * TS
                for h in range(NH):
                    p.wait("vector", T_, T_.c)
                    p.op("vector", lambda e, h=h, ci=ci: e.tensor_scalar(
                        SinSb[:], Sin[:, h, :], lam512[(h, ci)], None, ALU.mult), sem=V)
                    p.wait("sync", V, V.c)
                    p.dma("sync", qL, qt_t[:], qtil_i[h * 128:(h + 1) * 128, js:js + TS])
                    p.dma("sync", qL, olc[:, 0, :], oloc_i[h * DV:h * DV + 128, js:js + TS])
                    p.dma("sync", qL, olc[:, 1, :], oloc_i[h * DV + 128:h * DV + DV, js:js + TS])
                    p.dma("sync", qL, gsl[:, 0, :], gsil_i[h * DV:h * DV + 128, js:js + TS])
                    lvq = p.dma("sync", qL, gsl[:, 1, :], gsil_i[h * DV + 128:h * DV + DV, js:js + TS])
                    p.wait("tensor", qL, lvq)
                    p.wait("tensor", V, V.c)
                    for vb in range(2):
                        ppp = pA if vb == 0 else pB
                        p.op("tensor", lambda e, vb=vb, ppp=ppp: e.matmul(
                            ppp[:, 0:TS], SinSb[:, vb * 128:(vb + 1) * 128], qt_t[:],
                            start=True, stop=True), sem=(T_ if vb == 1 else None))
                    p.wait("vector", T_, T_.c)
                    for vb in range(2):
                        ppp = pA if vb == 0 else pB
                        p.op("vector", lambda e, vb=vb, ppp=ppp: e.tensor_add(
                            ostg[:, vb, :], olc[:, vb, :], ppp[:, 0:TS]),
                            sem=(V if vb == 1 else None))
                    vo = cx.V.c
                    p.wait("scalar", V, vo)
                    p.wait("scalar", T_, T_.c)
                    for vb in range(2):
                        p.op("scalar", lambda e, vb=vb: e.activation(o2b[:], ostg[:, vb, :], AF.Square), sem=SC)
                        p.wait("tensor", SC, SC.c)
                        p.op("tensor", lambda e, vb=vb: e.matmul(
                            prow[0:1, 0:TS], consts["onesv_b"][:, 0:1], o2b[:],
                            start=(vb == 0), stop=(vb == 1)), sem=T_)
                        p.wait("scalar", T_, T_.c)
                    p.wait("vector", T_, T_.c)
                    p.op("vector", lambda e: e.tensor_scalar(grow[0:1, :], prow[0:1, 0:TS],
                                                             EPS_LN, None, ALU.add), sem=V)
                    p.wait("scalar", V, V.c)
                    p.op("scalar", lambda e: e.activation(grow[0:1, :], grow[0:1, :], AF.Ln), sem=SC)
                    p.wait("scalar", SC, SC.c)
                    p.op("scalar", lambda e: e.activation(growb[0:1, :], grow[0:1, :], AF.Exp, scale=-0.5), sem=SC)
                    p.wait("tensor", SC, SC.c)
                    p.op("tensor", lambda e: e.matmul(pC[:, 0:TS], consts["ones1b"][0:1, :],
                                                      growb[0:1, :], start=True, stop=True), sem=T_)
                    p.wait("vector", T_, T_.c)
                    for vb in range(2):
                        p.op("vector", lambda e, vb=vb: e.tensor_mul(
                            ostg[:, vb, :], ostg[:, vb, :], pC[:, 0:TS]))
                        p.op("vector", lambda e, vb=vb, h=h: e.tensor_mul(
                            ogall[:, h * 2 + vb, :], ostg[:, vb, :], gsl[:, vb, :]),
                            sem=(V if vb == 1 else None))
                    p.wait("tensor", V, V.c)
                p.wait("tensor", V, V.c)
                for dc in range(DC):
                    p.wait("tensor", V, V.c)
                    for vc in range(16):
                        p.op("tensor", lambda e, vc=vc, dc=dc: e.matmul(
                            pD[:, 0:TS], wob[:, vc, dc * 128:(dc + 1) * 128], ogall[:, vc, :],
                            start=(vc == 0), stop=(vc == 15)), sem=(T_ if vc == 15 else None))
                    p.wait("vector", T_, T_.c)
                    p.op("vector", lambda e, dc=dc, js=js: e.tensor_add(
                        x[:, dc, js:js + TS], x[:, dc, js:js + TS], pD[:, 0:TS]), sem=V)
            cx.barrier()

        with ExitStack() as ph:
            cx.pR1 = cx.ps([1, TS], stack=ph); cx.pR2 = cx.ps([1, TS], stack=ph)
            bB = ln_bufs(cx, ph)
            bB["rstd_b"] = cx.sb([128, T], F32, ph)
            bB["mr_b"] = cx.sb([128, T], F32, ph)
            bB["lnscr"] = lnscr
            bB["xlnj"] = cx.sb([128, DC, TS], BF16, ph)
            bB["hb"] = cx.sb([128, FC, TS], BF16, ph)
            bB["w1b"] = cx.sb([128, 2, DC, 512], BF16, ph)
            bB["w2b"] = cx.sb([128, 2, FC, 128], BF16, ph)
            bB["hb2pp"] = cx.sb([128, DC], F32, ph)
            p.wait("scalar", qL, lcb)
            bB["sc_hb2pp"] = p.op("scalar", lambda e: e.mul(bB["hb2pp"][:, 0:DC], consts["b2pp"][:, 0:DC], 0.5), sem=SC)
            emit_ffn0(cx, x, bB, consts, t1_i, t2_i, pA, pB, None)
            cx.barrier()

        with ExitStack() as ph:
            xlnb = cx.sb([128, DC, T], BF16, ph)
            with ExitStack() as lnph:
                cx.pR1 = cx.ps([1, TS], stack=lnph); cx.pR2 = cx.ps([1, TS], stack=lnph)
                bC = ln_bufs(cx, lnph)
                bC["rstd_b"] = cx.sb([128, T], F32, lnph)
                bC["mr_b"] = cx.sb([128, T], F32, lnph)
                bC["lnscr"] = lnscr
                emit_ln_stats(cx, x, bC, consts, cx.pR1, cx.pR2)
                for j in range(NT):
                    emit_ln_norm_j(cx, x, xlnb, j, bC)
                cx.barrier()
            m = moe_bufs(cx, ph)
            m.update(pA=pA, pA2=pB, pD=pC, pD2=pD,
                     pC=cx.ps([128, TS], stack=ph), pC2=cx.ps([128, TS], stack=ph),
                     pT=cx.ps([128, 128], BF16, stack=ph),
                     gateT_dram=gateT_i, w1T_dram=w1T_i, w2T_dram=w2T_i, w3T_dram=w3T_i,
                     sw1T_dram=sw1T_i, sw2T_dram=sw2T_i, sw3T_dram=sw3T_i,
                     sc_scr=scscr)
            emit_moe(cx, x, xlnb, m, consts)
            p.wait("sync", V, V.c)
            p.dma("sync", ST, x4_o.rearrange("(dc q) t -> q dc t", q=128), x[:])
            cx.barrier()

        with ExitStack() as ph:
            xlnb = cx.sb([128, DC, T], BF16, ph)
            bD = dict(rstd_b=cx.sb([128, T], F32, ph), mr_b=cx.sb([128, T], F32, ph),
                     lnscr=lnscr)
            with ExitStack() as lnph:
                cx.pR1 = cx.ps([1, TS], stack=lnph); cx.pR2 = cx.ps([1, TS], stack=lnph)
                b2x = ln_bufs(cx, lnph)
                b2x.update(rstd_b=bD["rstd_b"], mr_b=bD["mr_b"], lnscr=lnscr)
                emit_ln_stats(cx, x, b2x, consts, cx.pR1, cx.pR2)
                bD["bcast_ql"] = b2x["bcast_ql"]
                for j in range(NT):
                    emit_ln_norm_j(cx, x, xlnb, j, b2x)
                p.dma("sync", ST, ln5_o[0:1, :], b2x["srow"][0:1, :])
                p.dma("sync", ST, ln5_o[1:2, :], b2x["mrow"][0:1, :])
                cx.barrier()

            wkb = cx.sb([128, DC, 128], BF16, ph)
            wqb = cx.sb([128, DC, 128], BF16, ph)
            wzb = cx.sb([128, DC, 128], BF16, ph)
            Kf = cx.sb([128, T], F32, ph)
            Vf = cx.sb([128, T], F32, ph)
            Qf = cx.sb([128, T], F32, ph)
            Zf = cx.sb([128, T], F32, ph)
            KCf = cx.sb([128, T], F32, ph)
            SCf = cx.sb([128, T], F32, ph)
            onescol = cx.sb([128, 1], F32, ph)
            paysb = cx.sb([128, 4, 8], F32, ph)
            p.op("vector", lambda e: e.memset(onescol[:], 1.0), sem=V)
            for ec in range(DC):
                p.wait("gpsimd", T_, T_.c)
                p.dma("gpsimd", gL, wkb[:], Wk1T_i.rearrange("(dc q) e -> q dc e", q=128)[:, :, ec * 128:(ec + 1) * 128])
                p.dma("gpsimd", gL, wqb[:], Wq1T_i.rearrange("(dc q) e -> q dc e", q=128)[:, :, ec * 128:(ec + 1) * 128])
                lw = p.dma("gpsimd", gL, wzb[:], Wk2T_i.rearrange("(dc q) e -> q dc e", q=128)[:, :, ec * 128:(ec + 1) * 128])
                p.wait("tensor", gL, lw)
                p.wait("tensor", G, G.c)
                p.wait("tensor", ST, ST.c)
                p.wait("scalar", G, G.c)
                p.wait("scalar", V, V.c)
                p.wait("scalar", ST, ST.c)
                for (wt, dstf, actf, biasc) in (
                    (wkb, Kf, AF.Exp, consts["bk1pp"]),
                    (wqb, Qf, AF.Sigmoid, consts["bq1pp"]),
                    (wzb, Zf, AF.Identity, None),
                ):
                    for j in range(NT):
                        js = j * TS
                        for dc in range(DC):
                            p.op("tensor", lambda e, dc=dc, js=js, wt=wt: e.matmul(
                                pA[:, 0:TS], wt[:, dc, :], xlnb[:, dc, js:js + TS],
                                start=(dc == 0), stop=(dc == DC - 1)), sem=(T_ if dc == DC - 1 else None))
                        p.wait("scalar", T_, T_.c)
                        if biasc is not None:
                            p.op("scalar", lambda e, dstf=dstf, js=js, actf=actf, biasc=biasc, ec=ec: e.activation(
                                dstf[0:128, js:js + TS], pA[:, 0:TS], actf,
                                bias=biasc[:, ec:ec + 1]), sem=SC)
                        else:
                            p.op("scalar", lambda e, dstf=dstf, js=js: e.copy(
                                dstf[0:128, js:js + TS], pA[:, 0:TS]), sem=SC)
                        p.wait("tensor", SC, SC.c)
                p.wait("vector", SC, SC.c)
                p.op("vector", lambda e, ec=ec: e.tensor_mul(Vf[:], x[:, ec, :], bD["rstd_b"][:]))
                p.op("vector", lambda e: e.tensor_sub(Vf[:], Vf[:], bD["mr_b"][:]), sem=V)
                p.wait("vector", V, V.c)
                p.op("vector", lambda e: e.tensor_mul(Vf[:], Kf[:], Vf[:]), sem=V)
                p.op("vector", lambda e: e.tensor_tensor_scan(
                    KCf[:], onescol[:].to_broadcast([128, T]), Kf[:], 0.0, ALU.mult, ALU.add), sem=G)
                p.op("vector", lambda e: e.tensor_tensor_scan(
                    SCf[:], onescol[:].to_broadcast([128, T]), Vf[:], 0.0, ALU.mult, ALU.add), sem=G)
                p.wait("vector", G, G.c)
                p.op("vector", lambda e, ec=ec: e.tensor_copy(paysb[:, 0, ec:ec + 1], KCf[:, T - 1:T]))
                p.op("vector", lambda e, ec=ec: e.tensor_copy(paysb[:, 1, ec:ec + 1], SCf[:, T - 1:T]))
                p.op("vector", lambda e, ec=ec: e.tensor_copy(paysb[:, 2, ec:ec + 1], Qf[:, T - 1:T]))
                p.op("vector", lambda e, ec=ec: e.tensor_copy(paysb[:, 3, ec:ec + 1], Zf[:, T - 1:T]), sem=V)
                p.wait("sync", V, V.c)
                p.wait("sync", G, G.c)
                p.wait("sync", SC, SC.c)
                p.dma("sync", ST, qf_o[ec * 128:(ec + 1) * 128, :], Qf[:])
                p.dma("sync", ST, kcum_o[ec * 128:(ec + 1) * 128, :], KCf[:])
                p.dma("sync", ST, scum_o[ec * 128:(ec + 1) * 128, :], SCf[:])
                p.wait("gpsimd", SC, SC.c)
                p.dma("gpsimd", ST, zb_o[ec * 128:(ec + 1) * 128, :], Zf[:])
            p.wait("sync", V, V.c)
            p.dma("sync", ST, payl_o, paysb[:])

        p.wait("sync", ST, ST.c)
        p.wait("gpsimd", ST, ST.c)
        p.emit(block)
    cx.stack.close()
    cx._semstack.close()
    return nc


K2SC = float(0.02 / np.sqrt(D))


def build_L3():
    nc = bass.Bass("TRN2", target_bir_lowering=False, debug=False)
    ap = lambda *a, **k: nc.dram_tensor(*a, **k).ap()
    x4_i = ap("x4", [D, T], F32, kind="ExternalInput")
    qf_i = ap("Qf", [D, T], F32, kind="ExternalInput")
    zb_i = ap("Zb", [D, T], BF16, kind="ExternalInput")
    kcum_i = ap("Kcum", [D, T], F32, kind="ExternalInput")
    scum_i = ap("Scum", [D, T], F32, kind="ExternalInput")
    ln5_i = ap("ln5", [2, T], F32, kind="ExternalInput")
    payall_i = ap("payall", [NC, 128, 4, 8], F32, kind="ExternalInput")
    prevp_i = ap("prevp", [128, 4, 8], F32, kind="ExternalInput")
    maskc_i = ap("maskc", [128, NC], F32, kind="ExternalInput")
    wpT_i = ap("WprojT", [D, D], F32, kind="ExternalInput")
    bpj_i = ap("bprojpp", [128, DC], F32, kind="ExternalInput")
    bk2_i = ap("bk2pp", [128, DC], F32, kind="ExternalInput")

    x5_o = ap("x5p", [D, T], F32, kind="ExternalOutput")
    q2_o = ap("Q2sh", [D, T], BF16, kind="ExternalOutput")
    t3_o = ap("t3", [128, 8], F32, kind="ExternalOutput")

    cx = Ctx(nc)
    p = cx.p
    qL, gL, T_, V, SC, G, ST = cx.qL, cx.gL, cx.T_, cx.V, cx.SC, cx.G, cx.ST

    x = cx.sb([128, DC, T])
    pA = cx.ps([128, TS]); pB = cx.ps([128, TS])
    rstd_b = cx.sb([128, T]); mr_b = cx.sb([128, T])
    paysbA = cx.sb([128, NC, 32])
    prevp = cx.sb([128, 4, 8])
    maskc = cx.sb([128, NC])
    pref = cx.sb([128, 32])
    obcol = cx.sb([128, 8])
    q2bcol = cx.sb([128, 8])
    tmpc = cx.sb([128, 8])
    bk2s = cx.sb([128, DC])
    bpjc = cx.sb([128, DC])
    t3sb = cx.sb([128, 8])
    onescol = cx.sb([128, 1])
    Kg = cx.sb([128, T]); Rr = cx.sb([128, T]); Sg = cx.sb([128, T])
    Qb = cx.sb([128, T]); Oar = cx.sb([128, T]); Vf = cx.sb([128, T])
    K2f = cx.sb([128, T]); S2f = cx.sb([128, T])
    zshb = cx.sb([128, T], BF16)
    q2shb = cx.sb([128, T], BF16)
    opb = cx.sb([128, DC, T], BF16)
    wpb = cx.sb([128, DC, 128], BF16)

    with (nc.Block() as block,):
        lx = p.dma("sync", qL, x[:], x4_i.rearrange("(dc q) t -> q dc t", q=128))
        p.dma("sync", qL, rstd_b[:], ln5_i[0:1, :].to_broadcast([128, T]))
        lr = p.dma("sync", qL, mr_b[:], ln5_i[1:2, :].to_broadcast([128, T]))
        p.dma("sync", qL, paysbA[:], payall_i.rearrange("c p f e -> p c (f e)"))
        p.dma("sync", qL, prevp[:], prevp_i)
        p.dma("sync", qL, maskc[:], maskc_i)
        lb = p.dma("sync", qL, bk2s[:], bk2_i)
        lbp = p.dma("sync", qL, bpjc[:], bpj_i)
        p.wait("vector", qL, lbp)
        p.op("vector", lambda e: e.memset(onescol[:], 1.0))
        p.op("vector", lambda e: e.memset(pref[:], 0.0), sem=V)
        p.wait("vector", V, V.c)
        for c_ in range(NC):
            p.op("vector", lambda e, c_=c_: e.scalar_tensor_tensor(
                pref[:], paysbA[:, c_, :], maskc[:, c_:c_ + 1], pref[:],
                ALU.mult, ALU.add), sem=V)
            p.wait("vector", V, V.c)
        p.op("vector", lambda e: e.tensor_scalar(tmpc[:, 0:8], pref[:, 0:8], 1e-6, None, ALU.add), sem=V)
        p.wait("vector", V, V.c)
        p.op("vector", lambda e: e.reciprocal(tmpc[:, 0:8], tmpc[:, 0:8]), sem=V)
        p.wait("vector", V, V.c)
        p.op("vector", lambda e: e.tensor_mul(obcol[:, 0:8], pref[:, 8:16], tmpc[:, 0:8]), sem=V)
        p.wait("vector", V, V.c)
        p.op("vector", lambda e: e.tensor_mul(obcol[:, 0:8], obcol[:, 0:8], prevp[:, 2, :]), sem=V)
        p.wait("vector", V, V.c)
        p.op("vector", lambda e: e.tensor_scalar(q2bcol[:, 0:8], prevp[:, 2, :], K2SC, None, ALU.mult), sem=V)
        p.wait("vector", V, V.c)

        for ec in range(DC):
            p.wait("sync", V, V.c)
            p.wait("sync", ST, ST.c)
            p.dma("sync", qL, Kg[:], kcum_i[ec * 128:(ec + 1) * 128, :])
            p.dma("sync", qL, Sg[:], scum_i[ec * 128:(ec + 1) * 128, :])
            p.dma("sync", qL, Qb[:], qf_i[ec * 128:(ec + 1) * 128, :])
            lvz = p.dma("sync", qL, zshb[:, 1:T], zb_i[ec * 128:(ec + 1) * 128, 0:T - 1])
            p.wait("vector", qL, lvz)
            p.op("vector", lambda e, ec=ec: e.tensor_scalar(
                Kg[:], Kg[:], pref[:, ec:ec + 1], 1e-6, ALU.add, op1=ALU.add), sem=V)
            p.wait("vector", V, V.c)
            p.op("vector", lambda e: e.reciprocal(Rr[:], Kg[:]), sem=V)
            p.op("vector", lambda e, ec=ec: e.tensor_scalar(
                Sg[:], Sg[:], pref[:, 8 + ec:9 + ec], None, ALU.add), sem=V)
            p.wait("vector", V, V.c)
            p.op("vector", lambda e: e.tensor_mul(Sg[:], Sg[:], Rr[:]), sem=V)
            p.wait("vector", V, V.c)
            p.op("vector", lambda e: e.tensor_mul(Oar[:], Sg[:], Qb[:]), sem=V)
            p.wait("vector", V, V.c)
            p.op("vector", lambda e, ec=ec: e.tensor_mul(Vf[:], x[:, ec, :], rstd_b[:]))
            p.op("vector", lambda e: e.tensor_sub(Vf[:], Vf[:], mr_b[:]), sem=V)
            p.wait("vector", V, V.c)
            p.op("vector", lambda e: e.tensor_sub(Vf[:, 1:T], Vf[:, 1:T], Oar[:, 0:T - 1]))
            p.op("vector", lambda e, ec=ec: e.tensor_sub(Vf[:, 0:1], Vf[:, 0:1], obcol[:, ec:ec + 1]), sem=V)
            p.wait("vector", V, V.c)
            p.op("vector", lambda e: e.tensor_mul(Vf[:, 0:1], Vf[:, 0:1], maskc[:, 0:1]))
            p.op("vector", lambda e, ec=ec: e.tensor_copy(zshb[:, 0:1], prevp[:, 3, ec:ec + 1]), sem=V)
            p.wait("scalar", V, V.c)
            p.op("scalar", lambda e, ec=ec: e.activation(
                K2f[:], zshb[:], AF.Sigmoid, bias=bk2s[:, ec:ec + 1], scale=K2SC), sem=SC)
            p.wait("vector", SC, SC.c)
            p.op("vector", lambda e: e.tensor_mul(K2f[:], K2f[:], Vf[:]), sem=V)
            p.wait("vector", V, V.c)
            p.op("vector", lambda e: e.tensor_tensor_scan(
                S2f[:], onescol[:, 0:1].to_broadcast([128, T]), K2f[:], 0.0,
                ALU.mult, ALU.add), sem=V)
            p.wait("vector", V, V.c)
            p.op("vector", lambda e, ec=ec: e.tensor_copy(t3sb[:, ec:ec + 1], S2f[:, T - 1:T]), sem=V)
            p.op("vector", lambda e: e.tensor_scalar(q2shb[:, 1:T], Qb[:, 0:T - 1], K2SC, None, ALU.mult))
            p.op("vector", lambda e, ec=ec: e.tensor_copy(q2shb[:, 0:1], q2bcol[:, ec:ec + 1]), sem=V)
            p.wait("vector", V, V.c)
            p.wait("sync", V, V.c)
            p.dma("sync", ST, q2_o[ec * 128:(ec + 1) * 128, :], q2shb[:])
            p.op("vector", lambda e: e.tensor_mul(S2f[:], S2f[:], q2shb[:]), sem=V)
            p.wait("vector", V, V.c)
            p.op("vector", lambda e, ec=ec: e.tensor_add(opb[:, ec, :], Oar[:], S2f[:]), sem=V)
        p.wait("tensor", V, V.c)
        for dc in range(DC):
            p.wait("gpsimd", T_, T_.c)
            lw = p.dma("gpsimd", gL, wpb[:], wpT_i.rearrange("(ec q) d -> q ec d", q=128)[:, :, dc * 128:(dc + 1) * 128])
            p.wait("tensor", gL, lw)
            for j in range(NT):
                js = j * TS
                p.wait("tensor", V, V.c)
                for ec in range(DC):
                    p.op("tensor", lambda e, ec=ec, js=js: e.matmul(
                        pA[:, 0:TS], wpb[:, ec, :], opb[:, ec, js:js + TS],
                        start=(ec == 0), stop=(ec == DC - 1)), sem=(T_ if ec == DC - 1 else None))
                p.wait("vector", T_, T_.c)
                p.op("vector", lambda e, dc=dc, js=js: e.scalar_tensor_tensor(
                    x[:, dc, js:js + TS], pA[:, 0:TS], bpjc[:, dc:dc + 1],
                    x[:, dc, js:js + TS], ALU.add, ALU.add), sem=V)
        p.wait("sync", V, V.c)
        p.dma("sync", ST, x5_o.rearrange("(dc q) t -> q dc t", q=128), x[:])
        p.dma("sync", ST, t3_o, t3sb[:])
        p.wait("sync", ST, ST.c)
        p.emit(block)
    cx.stack.close()
    cx._semstack.close()
    return nc


def build_L4():
    nc = bass.Bass("TRN2", target_bir_lowering=False, debug=False)
    ap = lambda *a, **k: nc.dram_tensor(*a, **k).ap()
    x5_i = ap("x5p", [D, T], F32, kind="ExternalInput")
    q2_i = ap("Q2sh", [D, T], BF16, kind="ExternalInput")
    t3all_i = ap("t3all", [NC, 128, 8], F32, kind="ExternalInput")
    maskc_i = ap("maskc", [128, NC], F32, kind="ExternalInput")
    wpT_i = ap("WprojT", [D, D], F32, kind="ExternalInput")
    gateT_i = ap("gateT", [D, 8], F32, kind="ExternalInput")
    ebias_i = ap("ebias", [128, 8], F32, kind="ExternalInput")
    w1T_i = ap("w1T", [8, D, D], F32, kind="ExternalInput")
    w2T_i = ap("w2T", [8, D, D], F32, kind="ExternalInput")
    w3T_i = ap("w3T", [8, D, D], F32, kind="ExternalInput")
    sw1T_i = ap("sw1T", [D, D], F32, kind="ExternalInput")
    sw2T_i = ap("sw2T", [D, D], F32, kind="ExternalInput")
    sw3T_i = ap("sw3T", [D, D], F32, kind="ExternalInput")
    onesd_ = ap("onesd", [128, 1], BF16, kind="ExternalInput")
    idb_ = ap("IDB", [128, 128], BF16, kind="ExternalInput")

    y_o = ap("y", [T, D], F32, kind="ExternalOutput")

    lnscr = ap("lnscr", [2, T], F32, kind="Internal")
    scscr = ap("scscr", [2, 8, T], F32, kind="Internal")

    cx = Ctx(nc)
    p = cx.p
    qL, gL, T_, V, SC, G, ST = cx.qL, cx.gL, cx.T_, cx.V, cx.SC, cx.G, cx.ST

    x = cx.sb([128, DC, T])
    consts = dict(
        onesd_b=cx.sb([128, 1], BF16), idb=cx.sb([128, 128], BF16),
        ebias=cx.sb([128, 8]),
    )
    pA = cx.ps([128, TS]); pB = cx.ps([128, TS])
    pC = cx.ps([128, TS]); pD = cx.ps([128, TS])

    with (nc.Block() as block,):
        lx = p.dma("sync", qL, x[:], x5_i.rearrange("(dc q) t -> q dc t", q=128))
        p.dma("sync", qL, consts["onesd_b"][:], onesd_)
        p.dma("sync", qL, consts["ebias"][:], ebias_i)
        lcb = p.dma("sync", qL, consts["idb"][:], idb_)
        p.wait("vector", qL, lcb)

        with ExitStack() as ph:
            t3a = cx.sb([128, NC, 8], F32, ph)
            maskc = cx.sb([128, NC], F32, ph)
            pref2 = cx.sb([128, 8], F32, ph)
            wpb2 = cx.sb([128, DC, D], BF16, ph)
            q2sb = cx.sb([128, DC, T], BF16, ph)
            p.dma("sync", qL, t3a[:], t3all_i.rearrange("c p e -> p c e"))
            lm = p.dma("sync", qL, maskc[:], maskc_i)
            p.dma("gpsimd", gL, wpb2[:], wpT_i.rearrange("(ec q) d -> q ec d", q=128))
            lq = p.dma("gpsimd", gL, q2sb[:], q2_i.rearrange("(ec q) t -> q ec t", q=128))
            p.wait("vector", qL, lm)
            p.op("vector", lambda e: e.memset(pref2[:], 0.0), sem=V)
            p.wait("vector", V, V.c)
            for c_ in range(NC):
                p.op("vector", lambda e, c_=c_: e.scalar_tensor_tensor(
                    pref2[:], t3a[:, c_, :], maskc[:, c_:c_ + 1], pref2[:],
                    ALU.mult, ALU.add), sem=V)
                p.wait("vector", V, V.c)
            p.wait("vector", gL, gL.c)
            for ec in range(DC):
                p.op("vector", lambda e, ec=ec: e.tensor_scalar(
                    wpb2[:, ec, :], wpb2[:, ec, :], pref2[:, ec:ec + 1], None, ALU.mult),
                    sem=(V if ec == DC - 1 else None))
            p.wait("tensor", V, V.c)
            p.wait("tensor", gL, lq)
            p.wait("vector", qL, lx)
            for dc in range(DC):
                for j in range(NT):
                    js = j * TS
                    p.wait("tensor", V, V.c)
                    for ec in range(DC):
                        p.op("tensor", lambda e, ec=ec, dc=dc, js=js: e.matmul(
                            pA[:, 0:TS], wpb2[:, ec, dc * 128:(dc + 1) * 128],
                            q2sb[:, ec, js:js + TS],
                            start=(ec == 0), stop=(ec == DC - 1)), sem=(T_ if ec == DC - 1 else None))
                    p.wait("vector", T_, T_.c)
                    p.op("vector", lambda e, dc=dc, js=js: e.tensor_add(
                        x[:, dc, js:js + TS], x[:, dc, js:js + TS], pA[:, 0:TS]), sem=V)
            cx.barrier()

        with ExitStack() as ph:
            xlnb = cx.sb([128, DC, T], BF16, ph)
            with ExitStack() as lnph:
                cx.pR1 = cx.ps([1, TS], stack=lnph); cx.pR2 = cx.ps([1, TS], stack=lnph)
                bC = ln_bufs(cx, lnph)
                bC["rstd_b"] = cx.sb([128, T], F32, lnph)
                bC["mr_b"] = cx.sb([128, T], F32, lnph)
                bC["lnscr"] = lnscr
                emit_ln_stats(cx, x, bC, consts, cx.pR1, cx.pR2)
                for j in range(NT):
                    emit_ln_norm_j(cx, x, xlnb, j, bC)
                cx.barrier()
            m = moe_bufs(cx, ph)
            m.update(pA=pA, pA2=pB, pD=pC, pD2=pD,
                     pC=cx.ps([128, TS], stack=ph), pC2=cx.ps([128, TS], stack=ph),
                     pT=cx.ps([128, 128], BF16, stack=ph),
                     gateT_dram=gateT_i, w1T_dram=w1T_i, w2T_dram=w2T_i, w3T_dram=w3T_i,
                     sw1T_dram=sw1T_i, sw2T_dram=sw2T_i, sw3T_dram=sw3T_i,
                     sc_scr=scscr)
            emit_moe(cx, x, xlnb, m, consts)
            cx.barrier()

        with ExitStack() as ph:
            hib = cx.sb([128, 128], BF16, ph)
            lob = cx.sb([128, 128], BF16, ph)
            stg = cx.sb([128, D], F32, ph)
            pT1 = cx.ps([128, 128], BF16, ph)
            pT2 = cx.ps([128, 128], BF16, ph)
            for tb in range(16):
                ts_ = tb * 128
                p.wait("vector", ST, ST.c)
                p.wait("scalar", ST, ST.c)
                for dc in range(DC):
                    p.wait("vector", T_, T_.c)
                    p.op("vector", lambda e, dc=dc, ts_=ts_: e.tensor_copy(
                        hib[:], x[:, dc, ts_:ts_ + 128]), sem=V)
                    p.wait("vector", V, V.c)
                    p.op("vector", lambda e, dc=dc, ts_=ts_: e.tensor_sub(
                        lob[:], x[:, dc, ts_:ts_ + 128], hib[:]), sem=V)
                    p.wait("tensor", V, V.c)
                    p.op("tensor", lambda e: e.transpose(pT1[:], hib[:], consts["idb"][:]))
                    p.op("tensor", lambda e: e.transpose(pT2[:], lob[:], consts["idb"][:]), sem=T_)
                    p.wait("scalar", T_, T_.c)
                    p.op("scalar", lambda e, dc=dc: e.copy(stg[:, dc * 128:(dc + 1) * 128], pT1[:]), sem=SC)
                    p.wait("vector", SC, SC.c)
                    p.op("vector", lambda e, dc=dc: e.tensor_add(
                        stg[:, dc * 128:(dc + 1) * 128], stg[:, dc * 128:(dc + 1) * 128], pT2[:]), sem=V)
                    p.wait("tensor", V, V.c)
                p.wait("sync", V, V.c)
                p.dma("sync", ST, y_o[ts_:ts_ + 128, :], stg[:])
        p.wait("sync", ST, ST.c)
        p.emit(block)
    cx.stack.close()
    cx._semstack.close()
    return nc


def _rope_tables(c):
    inv = 1.0 / (10000.0 ** (np.arange(0, DQK, 2, dtype=np.float64) / DQK))
    t = np.arange(c * T, (c + 1) * T, dtype=np.float64)[None, :]
    ang = inv[:, None] * t
    cos = np.cos(ang); sin = np.sin(ang)
    Cc = np.empty((128, T), np.float32); Ss = np.empty((128, T), np.float32)
    Cc[0::2] = cos; Cc[1::2] = cos
    Ss[0::2] = -sin; Ss[1::2] = sin
    return Cc, Ss


def _decay_consts():
    DT = np.zeros((NH, 4, 128, TS), np.float64)
    WIN = np.zeros((NH, 128, TS), np.float64)
    KH = np.zeros((NH, 4, 128), np.float64)
    i = np.arange(TS, dtype=np.float64)
    for h in range(NH):
        s = S_H[h]
        for jb in range(4):
            j = jb * 128 + np.arange(128, dtype=np.float64)
            diff = i[None, :] - j[:, None]
            DT[h, jb] = np.where(diff >= 0, np.exp2(diff * s) * SC_QK, 0.0)
            KH[h, jb] = np.exp2(s * (TS - 1 - j))
        WIN[h, :, :] = (np.exp2(s * (i + 1)) * SC_QK)[None, :]
    return DT.astype(BF), WIN.astype(BF), np.ascontiguousarray(KH.transpose(0, 2, 1)).astype(np.float32)


def make_l1_inputs(inp):
    x0 = np.asarray(inp["L0_attn_hidden_states"], np.float32).reshape(SEQ, D)
    x0T = np.ascontiguousarray(x0.T)
    W1T = np.ascontiguousarray(np.asarray(inp["L0_ffn_W1"], np.float32).T)
    W2T = np.ascontiguousarray(np.asarray(inp["L0_ffn_W2"], np.float32).T)
    b1pp = np.ascontiguousarray(np.asarray(inp["L0_ffn_b1"], np.float32).reshape(FC, 128).T)
    b2pp = np.ascontiguousarray(np.asarray(inp["L0_ffn_b2"], np.float32).reshape(DC, 128).T)
    DT, WIN, KH = _decay_consts()
    common = {
        "W1T": W1T, "W2T": W2T, "b1pp": b1pp, "b2pp": b2pp,
        "Wq": np.ascontiguousarray(np.asarray(inp["L0_attn_Wq"], np.float32)),
        "Wk": np.ascontiguousarray(np.asarray(inp["L0_attn_Wk"], np.float32)),
        "Wv": np.ascontiguousarray(np.asarray(inp["L0_attn_Wv"], np.float32)),
        "Wg": np.ascontiguousarray(np.asarray(inp["L0_attn_Wg"], np.float32)),
        "DT": DT, "WIN": WIN, "KHAT": KH,
        "IDB": np.eye(128, dtype=BF),
        "onesd": np.full((128, 1), 1.0 / D, BF),
        "ones1f": np.ones((1, 128), np.float32),
        "onesNf": np.full((128, 1), 1.0 / (D * FF), np.float32),
    }
    in_maps = []
    for c in range(NC):
        Cc, Ss = _rope_tables(c)
        m = dict(common)
        m["x0"] = np.ascontiguousarray(x0T[:, c * T:(c + 1) * T])
        m["COS"] = Cc; m["SIN"] = Ss
        in_maps.append(m)
    return in_maps


def make_l2_inputs(inp, r1res):
    f32 = lambda a: np.ascontiguousarray(np.asarray(a, np.float32))
    S_all = np.stack([r1res[c]["S_loc"] for c in range(NC)]).astype(np.float32)
    common = {
        "S_all": S_all,
        "Wo": f32(inp["L0_attn_Wo"]),
        "gnw2": np.ascontiguousarray(np.asarray(inp["L0_attn_g_norm_w"], np.float32).reshape(2, 128).T),
        "T1b": r1res[0]["T1b"], "T2b": r1res[0]["T2b"],
        "b1pp": np.ascontiguousarray(np.asarray(inp["L0_ffn_b1"], np.float32).reshape(FC, 128).T),
        "b2pp": np.ascontiguousarray(np.asarray(inp["L0_ffn_b2"], np.float32).reshape(DC, 128).T),
        "gateT": np.ascontiguousarray(np.asarray(inp["L1_ffn_gate_w"], np.float32).T),
        "ebias": np.ascontiguousarray(np.broadcast_to(np.asarray(inp["L1_ffn_expert_bias"], np.float32), (128, 8))),
        "w1T": np.ascontiguousarray(np.asarray(inp["L1_ffn_w1"], np.float32).transpose(0, 2, 1)),
        "w2T": np.ascontiguousarray(np.asarray(inp["L1_ffn_w2"], np.float32).transpose(0, 2, 1)),
        "w3T": np.ascontiguousarray(np.asarray(inp["L1_ffn_w3"], np.float32).transpose(0, 2, 1)),
        "sw1T": np.ascontiguousarray(np.asarray(inp["L1_ffn_sw1"], np.float32).T),
        "sw2T": np.ascontiguousarray(np.asarray(inp["L1_ffn_sw2"], np.float32).T),
        "sw3T": np.ascontiguousarray(np.asarray(inp["L1_ffn_sw3"], np.float32).T),
        "Wk1T": np.ascontiguousarray(np.asarray(inp["L1_attn_Wk1"], np.float32).T),
        "Wq1T": np.ascontiguousarray(np.asarray(inp["L1_attn_Wq1"], np.float32).T),
        "Wk2T": np.ascontiguousarray(np.asarray(inp["L1_attn_Wk2"], np.float32).T),
        "bk1pp": np.ascontiguousarray(np.asarray(inp["L1_attn_bk1"], np.float32).reshape(DC, 128).T),
        "bq1pp": np.ascontiguousarray(np.asarray(inp["L1_attn_bq1"], np.float32).reshape(DC, 128).T),
        "onesd": np.full((128, 1), 1.0 / D, BF),
        "onesv": np.full((128, 1), 1.0 / DV, BF),
        "ones1b": np.ones((1, 128), BF),
        "IDB": np.eye(128, dtype=BF),
    }
    in_maps = []
    for c in range(NC):
        W = np.zeros((NH, 128, NC), np.float64)
        for h in range(NH):
            for cp in range(c):
                W[h, :, cp] = 2.0 ** (S_H[h] * T * (c - 1 - cp))
        m = dict(common)
        m["WST"] = W.astype(np.float32)
        m["x1"] = r1res[c]["x1"]
        m["o_loc"] = r1res[c]["o_loc"]
        m["q_til"] = r1res[c]["q_til"]
        m["g_sil"] = r1res[c]["g_sil"]
        in_maps.append(m)
    return in_maps


def _maskc(c):
    m = np.zeros((128, NC), np.float32)
    m[:, :c] = 1.0
    return m


def make_l3_inputs(inp, r2res):
    payall = np.stack([r2res[c]["payl"] for c in range(NC)]).astype(np.float32)
    common = {
        "payall": payall,
        "WprojT": np.ascontiguousarray(np.asarray(inp["L1_attn_Wproj"], np.float32).T),
        "bprojpp": np.ascontiguousarray(np.asarray(inp["L1_attn_bproj"], np.float32).reshape(DC, 128).T),
        "bk2pp": np.ascontiguousarray(np.asarray(inp["L1_attn_bk2"], np.float32).reshape(DC, 128).T),
    }
    in_maps = []
    for c in range(NC):
        m = dict(common)
        m["x4"] = r2res[c]["x4"]; m["Qf"] = r2res[c]["Qf"]; m["Zb"] = r2res[c]["Zb"]
        m["Kcum"] = r2res[c]["Kcum"]; m["Scum"] = r2res[c]["Scum"]; m["ln5"] = r2res[c]["ln5"]
        m["prevp"] = r2res[c - 1]["payl"] if c > 0 else np.zeros((128, 4, 8), np.float32)
        m["maskc"] = _maskc(c)
        in_maps.append(m)
    return in_maps


def make_l4_inputs(inp, r3res):
    t3all = np.stack([r3res[c]["t3"] for c in range(NC)]).astype(np.float32)
    common = {
        "t3all": t3all,
        "WprojT": np.ascontiguousarray(np.asarray(inp["L1_attn_Wproj"], np.float32).T),
        "gateT": np.ascontiguousarray(np.asarray(inp["L1_ffn_gate_w"], np.float32).T),
        "ebias": np.ascontiguousarray(np.broadcast_to(np.asarray(inp["L1_ffn_expert_bias"], np.float32), (128, 8))),
        "w1T": np.ascontiguousarray(np.asarray(inp["L1_ffn_w1"], np.float32).transpose(0, 2, 1)),
        "w2T": np.ascontiguousarray(np.asarray(inp["L1_ffn_w2"], np.float32).transpose(0, 2, 1)),
        "w3T": np.ascontiguousarray(np.asarray(inp["L1_ffn_w3"], np.float32).transpose(0, 2, 1)),
        "sw1T": np.ascontiguousarray(np.asarray(inp["L1_ffn_sw1"], np.float32).T),
        "sw2T": np.ascontiguousarray(np.asarray(inp["L1_ffn_sw2"], np.float32).T),
        "sw3T": np.ascontiguousarray(np.asarray(inp["L1_ffn_sw3"], np.float32).T),
        "onesd": np.full((128, 1), 1.0 / D, BF),
        "IDB": np.eye(128, dtype=BF),
    }
    in_maps = []
    for c in range(NC):
        m = dict(common)
        m["x5p"] = r3res[c]["x5p"]; m["Q2sh"] = r3res[c]["Q2sh"]
        m["maskc"] = _maskc(c)
        in_maps.append(m)
    return in_maps


def kernel(**inp):
    nc1 = build_L1()
    r1 = run_bass_kernel_spmd(nc1, make_l1_inputs(inp), core_ids=list(range(NC)))
    r1res = [r1.results[c] for c in range(NC)]
    nc2 = build_L2()
    r2 = run_bass_kernel_spmd(nc2, make_l2_inputs(inp, r1res), core_ids=list(range(NC)))
    r2res = [r2.results[c] for c in range(NC)]
    nc3 = build_L3()
    r3 = run_bass_kernel_spmd(nc3, make_l3_inputs(inp, r2res), core_ids=list(range(NC)))
    r3res = [r3.results[c] for c in range(NC)]
    nc4 = build_L4()
    r4 = run_bass_kernel_spmd(nc4, make_l4_inputs(inp, r3res), core_ids=list(range(NC)))
    out = np.empty((SEQ, D), np.float32)
    for c in range(NC):
        out[c * T:(c + 1) * T, :] = r4.results[c]["y"]
    return out.reshape(1, SEQ, D)
